# version 21
# baseline (speedup 1.0000x reference)
import os
import numpy as np
import ml_dtypes
import contextlib
import concourse.bass as bass
import concourse.mybir as mybir
from concourse.bass_utils import run_bass_kernel_spmd

F32 = mybir.dt.float32
F16 = mybir.dt.float16
BF16 = mybir.dt.bfloat16
U8 = mybir.dt.uint8
AF = mybir.ActivationFunctionType
ALU = mybir.AluOpType
AX = mybir.AxisListType

T = 2048
D = 2048
NH = 16
HD = 128
NC_D = 16
NC_H = 44
OWN = 1024
NEG = -1e30
C_ATT = float(1.0 / np.sqrt(128.0))
NPAIR = 4
KLIMIT = int(os.environ.get("KLIMIT", "99"))
KPAIRS = int(os.environ.get("KPAIRS", str(NPAIR)))
KSUB = int(os.environ.get("KSUB", "9"))


class Sem:
    def __init__(self, h):
        self.h = h
        self.n = 0


class Plan:
    ENGS = ("sync", "gpsimd", "tensor", "vector", "scalar")

    def __init__(self):
        self.ops = {e: [] for e in self.ENGS}

    DRAIN = frozenset(("vector", "scalar", "gpsimd"))

    def emit(self, eng, waits, fn, inc=None, inc_by=1, drain=None):
        tok = None
        if inc is not None:
            inc.n += inc_by
            tok = inc.n
        if drain is None:
            drain = eng in self.DRAIN
        def op(e, waits=tuple(waits), fn=fn, inc=inc, inc_by=inc_by, drain=drain):
            for s, v in waits:
                if v > 0:
                    e.wait_ge(s.h, v)
            if drain:
                e.drain()
            ins = fn(e)
            if inc is not None:
                ins.then_inc(inc.h, inc_by)
        self.ops[eng].append(op)
        return tok


def build_program():
    nc = bass.Bass("TRN2", target_bir_lowering=False)

    dx = nc.dram_tensor("x", [T, D], F32, kind="ExternalInput").ap()
    dxT = nc.dram_tensor("xT", [D, OWN], F32, kind="ExternalInput").ap()
    dwqkv = nc.dram_tensor("wqkv", [NPAIR // 2, 128, 2, NC_D, 768], F32, kind="ExternalInput").ap()
    dwout = nc.dram_tensor("wout", [16, 128, NC_D, 128], F32, kind="ExternalInput").ap()
    dwg = nc.dram_tensor("wg8", [NC_H, 128, NC_D, 128], U8, kind="ExternalInput").ap()
    dwu = nc.dram_tensor("wu8", [NC_H, 128, NC_D, 128], U8, kind="ExternalInput").ap()
    dwd = nc.dram_tensor("wd8", [16, 128, NC_H, 128], U8, kind="ExternalInput").ap()
    dsg = nc.dram_tensor("sgc", [128, NC_H], F32, kind="ExternalInput").ap()
    dsu = nc.dram_tensor("suc", [128, NC_H], F32, kind="ExternalInput").ap()
    dsd = nc.dram_tensor("sdc", [128, 16], F32, kind="ExternalInput").ap()
    dqg = nc.dram_tensor("qgb", [128, 128], F32, kind="ExternalInput").ap()
    dkg = nc.dram_tensor("kgb", [128, 128], F32, kind="ExternalInput").ap()
    dcos = nc.dram_tensor("cosq", [128, 16, 128], F16, kind="ExternalInput").ap()
    dsin = nc.dram_tensor("ssin", [128, 16, 128], F16, kind="ExternalInput").ap()
    dkcol = nc.dram_tensor("kcol", [128, 2], F32, kind="ExternalInput").ap()
    ddiag = nc.dram_tensor("diagm", [128, 640], BF16, kind="ExternalInput").ap()
    dident = nc.dram_tensor("ident", [128, 128], BF16, kind="ExternalInput").ap()
    dout = nc.dram_tensor("out", [D, OWN], F32, kind="ExternalOutput").ap()
    dsend = nc.dram_tensor("sendb", [8, 128, 1024], BF16).ap()
    dgath = nc.dram_tensor("gathb", [2, 8, 128, 1024], BF16).ap()
    DEBUG = False
    if DEBUG:
        d_xnT = nc.dram_tensor("dbg_xnT", [128, NC_D * T], BF16, kind="ExternalOutput").ap()
        d_aoT = nc.dram_tensor("dbg_aoT", [128, NH * OWN], BF16, kind="ExternalOutput").ap()
    else:
        d_xnT = d_aoT = None

    P = Plan()
    with contextlib.ExitStack() as st:
        E = st.enter_context

        xnT = E(nc.sbuf_tensor("xnT", [128, NC_D * T], BF16))
        RS = E(nc.sbuf_tensor("RS", [128, 12416], F32))
        aoT = E(nc.sbuf_tensor("aoT", [128, NH * OWN], BF16))
        POOL = E(nc.sbuf_tensor("POOL", [128, 11264], F32))

        qgb = E(nc.sbuf_tensor("s_qgb", [128, 128], BF16))
        kgb = E(nc.sbuf_tensor("s_kgb", [128, 128], BF16))
        cosq = E(nc.sbuf_tensor("s_cosq", [128, 16 * 128], F16))
        ssin = E(nc.sbuf_tensor("s_ssin", [128, 16 * 128], F16))
        diagm = E(nc.sbuf_tensor("s_diagm", [128, 640], BF16))
        ident = E(nc.sbuf_tensor("s_ident", [128, 128], BF16))
        kcol = E(nc.sbuf_tensor("s_kcol", [128, 2], F32))
        ones_c = E(nc.sbuf_tensor("ones_c", [128, 1], BF16))
        ones_r = E(nc.sbuf_tensor("ones_r", [1, 128], BF16))
        sgc = E(nc.sbuf_tensor("s_sgc", [128, NC_H], F32))
        suc = E(nc.sbuf_tensor("s_suc", [128, NC_H], F32))
        sdc = E(nc.sbuf_tensor("s_sdc", [128, 16], F32))
        stat = E(nc.sbuf_tensor("stat", [128, 64], F32))
        csx_hi = E(nc.sbuf_tensor("csx_hi", [1, 1024], BF16))
        csx_lo = E(nc.sbuf_tensor("csx_lo", [1, 1024], BF16))
        csh_hi = E(nc.sbuf_tensor("csh_hi", [1, 512], BF16))
        csh_lo = E(nc.sbuf_tensor("csh_lo", [1, 512], BF16))

        psA = E(nc.psum_tensor("psA", [128, 2048], F32))
        psB = E(nc.psum_tensor("psB", [128, 1024], F32))
        psT = E(nc.psum_tensor("psT", [128, 2048], BF16))

        pe = Sem(E(nc.semaphore("pe")))
        ve = Sem(E(nc.semaphore("ve")))
        se = Sem(E(nc.semaphore("se")))
        gp = Sem(E(nc.semaphore("gp")))
        s_cst = Sem(E(nc.semaphore("s_cst")))
        s_cos = Sem(E(nc.semaphore("s_cos")))
        s_x = [Sem(E(nc.semaphore("s_x0"))), Sem(E(nc.semaphore("s_x1")))]
        s_slab = Sem(E(nc.semaphore("s_slab")))
        s_wg = [Sem(E(nc.semaphore("s_wg0"))), Sem(E(nc.semaphore("s_wg1")))]
        s_wd = [Sem(E(nc.semaphore("s_wd0"))), Sem(E(nc.semaphore("s_wd1")))]
        s_wo = Sem(E(nc.semaphore("s_wo")))
        s_xf = [Sem(E(nc.semaphore("s_xf0"))), Sem(E(nc.semaphore("s_xf1")))]
        s_out = [Sem(E(nc.semaphore("s_out0"))), Sem(E(nc.semaphore("s_out1")))]
        s_snd = [Sem(E(nc.semaphore("s_snd0"))), Sem(E(nc.semaphore("s_snd1")))]
        s_rcv = Sem(E(nc.semaphore("s_rcv")))

        RS_bf = RS.ap().bitcast(BF16)
        POOL_bf = POOL.ap().bitcast(BF16)

        KP, VP, QP = 0, 4096, 8192
        SSTG = 12288
        PT0 = 14336
        SCRP = [7680, 8448]
        KAP = [9216, 9728]
        QAP = [10240, 10752]
        HID0 = 0
        PH0 = 8192

        def bc2(tbl, t, lo, n):
            return bass.AP(tbl.ap().tensor, t * 128 + lo, [[2048, 128], [0, 2], [1, n]])
        XF0 = 10880
        SILU_G = 11904
        csx = RS[0:1, XF0:XF0 + 1024]
        csh = RS[0:1, SILU_G:SILU_G + 512]

        def dma(eng, sem, out_ap, in_ap, waits=()):
            w = list(waits)
            if sem.n > 0:
                w.append((sem, sem.n))
            return P.emit(eng, w,
                          lambda e, o=out_ap, i=in_ap: e.dma_start(out=o, in_=i),
                          inc=sem, inc_by=16, drain=False)

        tb_pe = [0, 0]
        tb_ve = [0, 0]
        tb_next = [0]

        def transpose_to(src_bf_ap_fn, dst_fn, waits, width=128, evac="vector"):
            b = tb_next[0]
            tb_next[0] ^= 1
            wt = list(waits)
            es = ve if evac == "vector" else se
            if tb_ve[b] != 0:
                wt.append(tb_ve[b])
            tok_t = P.emit("tensor", wt,
                           lambda e, b_=b: e.transpose(psT[:, b_ * 1024: b_ * 1024 + width],
                                                       src_bf_ap_fn(), ident[:, :]),
                           inc=pe)
            tb_pe[b] = tok_t
            if evac == "vector":
                tok_v = P.emit("vector", [(pe, tok_t)],
                               lambda e, b_=b: e.tensor_copy(dst_fn(), psT[:, b_ * 1024: b_ * 1024 + width]),
                               inc=ve, drain=False)
            else:
                tok_v = P.emit("scalar", [(pe, tok_t)],
                               lambda e, b_=b: e.copy(dst_fn(), psT[:, b_ * 1024: b_ * 1024 + width]),
                               inc=se, drain=False)
            tb_ve[b] = (es, tok_v)
            return tok_t, tok_v

        t_const = 0
        for o, i in ((qgb[:, :], dqg), (kgb[:, :], dkg), (diagm[:, :], ddiag), (kcol[:, :], dkcol),
                     (ident[:, :], dident),
                     (sgc[:, :], dsg), (suc[:, :], dsu), (sdc[:, :], dsd)):
            t_const = dma("gpsimd", s_cst, o, i)
        t_cos = dma("sync", s_cos, cosq[:, :], dcos)
        t_cos = dma("sync", s_cos, ssin[:, :], dsin)
        P.emit("gpsimd", [], lambda e: e.memset(ones_c[:, :], 1.0), inc=gp)
        t_ones = P.emit("gpsimd", [], lambda e: e.memset(ones_r[:, :], 1.0), inc=gp)
        for eng in ("tensor", "vector", "scalar"):
            P.emit(eng, [(s_cst, t_const), (gp, t_ones), (s_cos, t_cos)], lambda e: e.nop())

        ln_apply_tok = [0, 0]
        ln_sm_tok = [0, 0]
        ln_sqr_read = [0, 0]
        slab_pe = [0, 0]

        for t in range(16):
            pg = t % 2
            xc = pg * 2048
            sqc = 4096 + pg * 2048
            sb = pg * 10
            w = []
            if t >= 2:
                w = [(se, ln_apply_tok[pg]), (ve, ln_sm_tok[pg])]
            tok_x = dma("sync", s_x[pg], RS[:, xc:xc + 2048],
                        dx.rearrange("(n p) f -> p n f", p=128)[:, t, :], waits=w)
            tok_sq = P.emit("scalar", [(s_x[pg], tok_x)],
                            lambda e, a=sqc, bcol=xc: e.activation(RS[:, a:a + 2048], RS[:, bcol:bcol + 2048], AF.Square),
                            inc=se, drain=False)
            ln_sm_tok[pg] = P.emit("vector", [(s_x[pg], tok_x)],
                                   lambda e, bcol=xc, s=sb: e.reduce_sum(stat[:, s:s + 1], RS[:, bcol:bcol + 2048], axis=AX.X),
                                   inc=ve)
            P.emit("vector", [(se, tok_sq)],
                   lambda e, a=sqc, s=sb: e.reduce_sum(stat[:, s + 1:s + 2], RS[:, a:a + 2048], axis=AX.X), inc=ve)
            P.emit("vector", [], lambda e, s=sb: e.tensor_scalar(stat[:, s + 2:s + 3], stat[:, s:s + 1], 1.0 / 2048.0, None, ALU.mult), inc=ve)
            P.emit("vector", [], lambda e, s=sb: e.tensor_scalar(stat[:, s + 3:s + 4], stat[:, s + 1:s + 2], 1.0 / 2048.0, None, ALU.mult), inc=ve)
            P.emit("vector", [], lambda e, s=sb: e.tensor_tensor(stat[:, s + 4:s + 5], stat[:, s + 2:s + 3], stat[:, s + 2:s + 3], ALU.mult), inc=ve)
            P.emit("vector", [], lambda e, s=sb: e.tensor_tensor(stat[:, s + 5:s + 6], stat[:, s + 3:s + 4], stat[:, s + 4:s + 5], ALU.subtract), inc=ve)
            tok_v = P.emit("vector", [], lambda e, s=sb: e.tensor_scalar(stat[:, s + 6:s + 7], stat[:, s + 5:s + 6], 1e-5, None, ALU.add), inc=ve)
            wv = [(ve, tok_v)]
            if t >= 2:
                wv.append((ve, ln_sqr_read[pg]))
            tok_sqr = P.emit("scalar", wv, lambda e, s=sb: e.activation(stat[:, s + 7:s + 8], stat[:, s + 6:s + 7], AF.Sqrt), inc=se)
            tok_rs = P.emit("vector", [(se, tok_sqr)], lambda e, s=sb: e.reciprocal(stat[:, s + 8:s + 9], stat[:, s + 7:s + 8]), inc=ve)
            ln_sqr_read[pg] = tok_rs
            tok_bias = P.emit("vector", [],
                              lambda e, s=sb: e.tensor_scalar(stat[:, s + 9:s + 10], stat[:, s + 2:s + 3], stat[:, s + 8:s + 9], -1.0, ALU.mult, ALU.mult),
                              inc=ve)
            xcol = pg * 2048
            w = [(ve, tok_bias)]
            if t >= 2:
                w.append((pe, slab_pe[pg]))
            ln_apply_tok[pg] = P.emit(
                "scalar", w,
                lambda e, bcol=xc, xo=xcol, s=sb: e.activation(POOL_bf[:, xo:xo + 2048], RS[:, bcol:bcol + 2048], AF.Identity,
                                                               bias=stat[:, s + 9:s + 10], scale=stat[:, s + 8:s + 9]),
                inc=se)
            for c in range(NC_D):
                tok_t, _ = transpose_to(
                    (lambda xo=xcol, c_=c: POOL_bf[:, xo + c_ * 128: xo + c_ * 128 + 128]),
                    (lambda c_=c, t_=t: xnT[:, c_ * T + t_ * 128: c_ * T + t_ * 128 + 128]),
                    [(se, ln_apply_tok[pg])])
            slab_pe[pg] = tok_t
        xnT_tok = max((x[1] if x != 0 else 0) for x in tb_ve)

        for tb in range(2 if KLIMIT >= 2 else 0):
            for c in range(NC_D):
                P.emit("tensor", [(ve, xnT_tok), (gp, t_ones)] if c == 0 else [],
                       lambda e, c_=c, tb_=tb: e.matmul(
                           psB[0:1, tb_ * 512: tb_ * 512 + 512], ones_c[:, :],
                           xnT[:, c_ * T + tb_ * 512: c_ * T + tb_ * 512 + 512],
                           start=(c_ == 0), stop=(c_ == NC_D - 1)),
                       inc=pe if c == NC_D - 1 else None)
            P.emit("vector", [(pe, pe.n)],
                   lambda e, tb_=tb: e.tensor_copy(csx[:, tb_ * 512: tb_ * 512 + 512], psB[0:1, tb_ * 512: tb_ * 512 + 512]),
                   inc=ve)
        tok_n = P.emit("vector", [], lambda e: e.tensor_scalar(csx[:, :], csx[:, :], -128.0, None, ALU.mult), inc=ve)
        tok_hi = P.emit("vector", [(ve, tok_n)], lambda e: e.tensor_copy(csx_hi[:, :], csx[:, :]), inc=ve)
        csx_tok = P.emit("vector", [(ve, tok_hi)], lambda e: e.tensor_tensor(csx_lo[:, :], csx[:, :], csx_hi[:, :], ALU.subtract), inc=ve)
        csum_pe = pe.n

        WSLAB = 24576
        wslab_tok = [0]
        wslab_free_pe = [0]

        qkv_free_ve = [0, 0]
        qkv_free_se = [0, 0]
        qkv_free_gp = [0, 0]

        rmsK_gp = [0, 0]
        rmsK_ve = [0, 0]
        rmsQ_ve = [0, 0]
        khat_gp = [0, 0]
        att_free_se = 0
        att_free_ve = 0
        att_free_sa = [0, 0]
        att_free_sb = [0]
        phat_pe = [0, 0, 0, 0]
        phat_ve = [0, 0, 0, 0]
        o_pe = [0, 0]
        o_ve = [0, 0]
        pt_pe = [0, 0]
        pt_ve = [0, 0]
        rms_it = [0]
        scr_ve = [0, 0]
        scr_gp = [0, 0]
        kh_pe = [0]
        qh_pe = [0]
        snd_tok = [0, 0]

        att_done_ve = 0

        def qkv_evac(t, pcol, tok_mm):
            SCR = SCRP[t % 2]
            KA = KAP[t % 2]
            QA = QAP[t % 2]
            KH = SCR * 2
            QH = SCR * 2 + 1024
            tok_kh = 0
            tok_qh = 0
            wcp = [(pe, tok_mm)]
            if scr_ve[t % 2] > 0:
                wcp += [(ve, scr_ve[t % 2]), (gp, scr_gp[t % 2])]
            tok_cp = P.emit("scalar", wcp,
                            lambda e, pc=pcol, S=SCR: e.copy(POOL[:, S:S + 768], psA[:, pc:pc + 768]),
                            inc=se, drain=False)
            if KSUB >= 3:
                sqk = 16 + (t % 2) * 16
                P.emit("vector", [(se, tok_cp)],
                       lambda e, t_=t: e.tensor_copy(
                           POOL_bf[:, VP:VP + 4096].rearrange("p (h f) -> p h f", h=2)[:, :, t_ * 128:t_ * 128 + 128],
                           POOL[:, SCR + 256:SCR + 512].rearrange("p (h f) -> p h f", h=2)),
                       inc=ve, drain=False)
                if KSUB >= 4:
                    nsl = 4
                    wk = [(gp, scr_gp[0])] if scr_gp[0] > 0 else []
                    P.emit("vector", wk,
                           lambda e: e.tensor_tensor(POOL[:, KA:KA + 256], POOL[:, SCR:SCR + 256],
                                                     POOL[:, SCR:SCR + 256], ALU.mult), inc=ve, drain=False)
                    P.emit("vector", [],
                           lambda e, s=sqk: e.reduce_sum(stat[:, s:s + 2],
                                                         POOL[:, KA:KA + 256].rearrange("p (h f) -> p h f", h=2),
                                                         axis=AX.X), inc=ve)
                    if True:
                        P.emit("vector", [],
                               lambda e: e.tensor_tensor(POOL[:, QA:QA + 256], POOL[:, SCR + 512:SCR + 768],
                                                         POOL[:, SCR + 512:SCR + 768], ALU.mult), inc=ve, drain=False)
                        P.emit("vector", [],
                               lambda e, s=sqk: e.reduce_sum(stat[:, s + 2:s + 4],
                                                             POOL[:, QA:QA + 256].rearrange("p (h f) -> p h f", h=2),
                                                             axis=AX.X), inc=ve)
                    P.emit("vector", [],
                           lambda e, s=sqk: e.tensor_scalar(
                               stat[:, s + 4:s + 6], stat[:, s:s + 2], 1.0 / 128.0, 1e-6,
                               ALU.mult, ALU.add), inc=ve)
                    tok_st = P.emit("vector", [],
                                    lambda e, s=sqk: e.tensor_scalar(
                                        stat[:, s + 6:s + 8], stat[:, s + 2:s + 4], 1.0, 128e-6,
                                        ALU.mult, ALU.add), inc=ve, drain=False)
                    tok_sqr = P.emit("scalar", [(ve, tok_st)],
                                     lambda e, s=sqk, n=nsl: e.activation(stat[:, s + 8:s + 8 + n],
                                                                          stat[:, s + 4:s + 4 + n], AF.Sqrt),
                                     inc=se, drain=False)
                    tok_rc = P.emit("vector", [(se, tok_sqr)],
                                    lambda e, s=sqk, n=nsl: e.reciprocal(stat[:, s + 12:s + 12 + n],
                                                                         stat[:, s + 8:s + 8 + n]), inc=ve)
                    for h in (0, 1):
                        P.emit("vector", [],
                               lambda e, h_=h, s=sqk: e.scalar_tensor_tensor(
                                   POOL[:, SCR + h_ * 128: SCR + h_ * 128 + 128],
                                   POOL[:, SCR + h_ * 128: SCR + h_ * 128 + 128],
                                   stat[:, s + 12 + h_:s + 13 + h_], kgb[:, :], ALU.mult, ALU.mult),
                               inc=ve, drain=(h == 0))
                        if True:
                            P.emit("vector", [],
                                   lambda e, h_=h, s=sqk: e.scalar_tensor_tensor(
                                       POOL[:, SCR + 512 + h_ * 128: SCR + 512 + h_ * 128 + 128],
                                       POOL[:, SCR + 512 + h_ * 128: SCR + 512 + h_ * 128 + 128],
                                       stat[:, s + 14 + h_:s + 15 + h_], qgb[:, :], ALU.mult, ALU.mult),
                                   inc=ve, drain=False)
                    tok_g1 = ve.n
                    if KSUB >= 5:
                        for h in (0, 1):
                            pass
                        wk = [(ve, tok_g1)]
                        if kh_pe[0] > 0:
                            wk.append((pe, kh_pe[0]))
                        P.emit("gpsimd", wk,
                               lambda e, t_=t: e.tensor_tensor(
                                   POOL[:, KA:KA + 256].rearrange("p (h f) -> p h f", h=2),
                                   POOL[:, SCR:SCR + 256].rearrange("p (h f) -> p h f", h=2),
                                   bc2(cosq, t_, 0, 128), ALU.mult), inc=gp, drain=False)
                        P.emit("gpsimd", [],
                               lambda e, t_=t: e.tensor_tensor(
                                   POOL[:, KA + 256:KA + 512].rearrange("p (h f) -> p h f", h=2)[:, :, 0:64],
                                   POOL[:, SCR:SCR + 256].rearrange("p (h f) -> p h f", h=2)[:, :, 64:128],
                                   bc2(ssin, t_, 0, 64),
                                   ALU.mult), inc=gp, drain=False)
                        P.emit("gpsimd", [],
                               lambda e, t_=t: e.tensor_tensor(
                                   POOL[:, KA + 256:KA + 512].rearrange("p (h f) -> p h f", h=2)[:, :, 64:128],
                                   POOL[:, SCR:SCR + 256].rearrange("p (h f) -> p h f", h=2)[:, :, 0:64],
                                   bc2(ssin, t_, 64, 64),
                                   ALU.mult), inc=gp, drain=False)
                        tok_kh = P.emit("gpsimd", [(pe, kh_pe[0])] if kh_pe[0] > 0 else [],
                                        lambda e: e.tensor_tensor(
                                            POOL_bf[:, KH:KH + 256],
                                            POOL[:, KA:KA + 256], POOL[:, KA + 256:KA + 512], ALU.add),
                                        inc=gp)
                        khat_gp[0] = tok_kh
                        if KSUB >= 7:
                            P.emit("vector", [],
                                   lambda e, t_=t: e.tensor_tensor(
                                       POOL[:, QA:QA + 256].rearrange("p (h f) -> p h f", h=2),
                                       POOL[:, SCR + 512:SCR + 768].rearrange("p (h f) -> p h f", h=2),
                                       bc2(cosq, t_, 0, 128), ALU.mult), inc=ve)
                            P.emit("vector", [],
                                   lambda e, t_=t: e.tensor_tensor(
                                       POOL[:, QA + 256:QA + 512].rearrange("p (h f) -> p h f", h=2)[:, :, 0:64],
                                       POOL[:, SCR + 512:SCR + 768].rearrange("p (h f) -> p h f", h=2)[:, :, 64:128],
                                       bc2(ssin, t_, 0, 64),
                                       ALU.mult), inc=ve, drain=False)
                            P.emit("vector", [],
                                   lambda e, t_=t: e.tensor_tensor(
                                       POOL[:, QA + 256:QA + 512].rearrange("p (h f) -> p h f", h=2)[:, :, 64:128],
                                       POOL[:, SCR + 512:SCR + 768].rearrange("p (h f) -> p h f", h=2)[:, :, 0:64],
                                       bc2(ssin, t_, 64, 64),
                                       ALU.mult), inc=ve, drain=False)
                            tok_qh = P.emit("vector", [(pe, qh_pe[0])] if qh_pe[0] > 0 else [],
                                            lambda e: e.tensor_tensor(
                                                POOL_bf[:, QH:QH + 256],
                                                POOL[:, QA:QA + 256], POOL[:, QA + 256:QA + 512], ALU.add),
                                            inc=ve)
                        scr_ve[t % 2] = ve.n
            scr_gp[t % 2] = gp.n

            def flush(t_=t, KH_=KH, QH_=QH, tok_kh_=tok_kh if KSUB >= 5 else 0,
                      tok_qh_=tok_qh if KSUB >= 7 else 0):
                if KSUB >= 6:
                    for h in (0, 1):
                        tk_t, _ = transpose_to(
                            (lambda h_=h: POOL_bf[:, KH_ + h_ * 128:KH_ + h_ * 128 + 128]),
                            (lambda h_=h: POOL_bf[:, KP + h_ * 2048 + t_ * 128: KP + h_ * 2048 + t_ * 128 + 128]),
                            [(gp, tok_kh_)], evac="scalar")
                        kh_pe[0] = tk_t
                if KSUB >= 7:
                    for h in (0, 1):
                        tq_t, _ = transpose_to(
                            (lambda h_=h: POOL_bf[:, QH_ + h_ * 128:QH_ + h_ * 128 + 128]),
                            (lambda h_=h: POOL_bf[:, QP + h_ * 2048 + t_ * 128: QP + h_ * 2048 + t_ * 128 + 128]),
                            [(ve, tok_qh_)], evac="scalar")
                        qh_pe[0] = tq_t
                scr_ve[t_ % 2] = ve.n
            return tok_cp, flush

        for pair in range(KPAIRS if KLIMIT >= 25 else 0):
            pgrp, pl = pair // 2, pair % 2
            if pl == 0:
                w = [(pe, wslab_free_pe[0])] if pair > 0 else [(se, ln_apply_tok[0]), (se, ln_apply_tok[1]), (ve, ln_sm_tok[0]), (ve, ln_sm_tok[1]), (ve, csx_tok)]
                wslab_tok[0] = dma("gpsimd", s_slab, RS_bf[:, 0:WSLAB],
                                   dwqkv[pgrp], waits=w)
            wb = pl * 12288

            pendT = []
            for t in range(16 if KSUB >= 2 else 0):
                ping = t % 2
                pcol = ping * 1024
                w = [(s_slab, wslab_tok[0]), (ve, xnT_tok)]
                if qkv_free_se[ping] > 0:
                    w.append((se, qkv_free_se[ping]))
                for c in range(NC_D):
                    def mk1(e, c_=c, pc=pcol, wb_=wb, t_=t):
                        e.ldweights(xnT[:, c_ * T + t_ * 128: c_ * T + t_ * 128 + 128])
                        return e.matmul(psA[:, pc:pc + 512],
                                        xnT[:, c_ * T + t_ * 128: c_ * T + t_ * 128 + 128],
                                        RS_bf[:, wb_ + c_ * 768: wb_ + c_ * 768 + 512],
                                        start=(c_ == 0), stop=(c_ == NC_D - 1))
                    P.emit("tensor", w if c == 0 else [], mk1, inc=None)
                    if True:
                        def mk2b(e, c_=c, pc=pcol, wb_=wb, t_=t):
                            return e.matmul(psA[:, pc + 512:pc + 768],
                                            xnT[:, c_ * T + t_ * 128: c_ * T + t_ * 128 + 128],
                                            RS_bf[:, wb_ + c_ * 768 + 512: wb_ + c_ * 768 + 768],
                                            start=(c_ == 0), stop=(c_ == NC_D - 1))
                        P.emit("tensor", [], mk2b,
                               inc=pe if c == NC_D - 1 else None)
                tok_mm = pe.n
                if pl == 1 and t == 15:
                    wslab_free_pe[0] = tok_mm

                if pendT:
                    pendT.pop(0)()
                tok_cp, tok_fl = qkv_evac(t, pcol, tok_mm)
                pendT.append(tok_fl)
                qkv_free_se[ping] = tok_cp
            while pendT:
                pendT.pop(0)()
            qkv_done_ve = ve.n
            qkv_done_se = se.n

            for h in ((0, 1) if KLIMIT >= 30 else ()):
                qb = QP + h * 2048
                kb = KP + h * 2048
                vb = VP + h * 2048
                hloc = pair * 2 + h

                def emit_softmax_pre(i):
                    nonlocal att_free_ve
                    slot = i % 4
                    hf = i // 8
                    dri = (i % 8) // 4 + 1
                    nv = dri + 2
                    sub = i % 4
                    w1 = (dri - 1) * 512 + sub * 128
                    w2 = dri * 512 - w1
                    sa = 40 + (i % 2) * 12
                    w = [(ve, qkv_done_ve), (se, qkv_done_se)]
                    if att_free_ve > 0:
                        w.append((ve, att_free_ve))
                    for v in range(nv):
                        if v < dri:
                            kcolv = (hf * 2 + v) * 512
                        else:
                            kcolv = ((1 - hf) * 2 + (v - dri)) * 512
                        def mks(e, i_=i, v_=v, kc=kcolv, qb_=qb, kb_=kb):
                            if v_ == 0:
                                e.ldweights(POOL_bf[:, qb_ + i_ * 128: qb_ + i_ * 128 + 128])
                            return e.matmul(psA[:, v_ * 512: v_ * 512 + 512],
                                            POOL_bf[:, qb_ + i_ * 128: qb_ + i_ * 128 + 128],
                                            POOL_bf[:, kb_ + kc: kb_ + kc + 512],
                                            start=True, stop=True)
                        wv_ = list(w) if v == 0 else []
                        if att_free_sa[0] > 0 and v <= att_free_sa[1]:
                            wv_.append((se, att_free_sa[0] if v < att_free_sa[1] else att_free_sb[0]))
                        elif att_free_sb[0] > 0 and v > att_free_sa[1]:
                            wv_.append((se, att_free_sb[0]))
                        tok_mmv = P.emit("tensor", wv_, mks,
                                         inc=pe if v == nv - 1 else None)
                    tok_mx = P.emit("vector", [(pe, tok_mmv)],
                                    lambda e, s=sa, n_=nv: e.reduce_max(
                                        stat[:, s + 4:s + 5], psA[:, 0:n_ * 512],
                                        axis=AX.X, negate=True),
                                    inc=ve, drain=False)
                    att_free_ve = tok_mx
                    wslot = [(ve, tok_mx)]
                    if phat_pe[slot] > 0:
                        wslot.append((pe, phat_pe[slot]))
                    tok_e1a = 0
                    if w1 > 0:
                        tok_e1a = P.emit("scalar", wslot,
                               lambda e, s_=slot, w1_=w1, s=sa: e.activation(
                                   aoT[:, PH0 + s_ * 2048: PH0 + s_ * 2048 + w1_],
                                   psA[:, 0:w1_], AF.Exp, bias=stat[:, s + 4:s + 5], scale=1.0,
                                   accum_out=stat[:, s + 6:s + 7]),
                               inc=se, drain=False)
                    tok_e1t = P.emit("scalar", wslot if w1 == 0 else [],
                                     lambda e, s_=slot, w1_=w1, dr_=dri, s=sa: e.activation(
                                         aoT[:, PH0 + s_ * 2048 + w1_: PH0 + s_ * 2048 + dr_ * 512],
                                         psA[:, w1_:dr_ * 512], AF.Exp, bias=stat[:, s + 4:s + 5], scale=1.0),
                                     inc=se, drain=False)
                    tok_e1 = P.emit("scalar", [],
                                    lambda e, s_=slot, dr_=dri, nv_=nv, s=sa: e.activation(
                                        aoT[:, PH0 + s_ * 2048 + dr_ * 512: PH0 + s_ * 2048 + nv_ * 512],
                                        psA[:, dr_ * 512:nv_ * 512], AF.Exp, bias=stat[:, s + 4:s + 5], scale=1.0,
                                        accum_out=stat[:, s + 7:s + 8]),
                                    inc=se, drain=False)
                    tok_mk = P.emit("gpsimd", [(se, tok_e1t)],
                           lambda e, s_=slot, w1_=w1, w2_=w2, dr_=dri: e.tensor_tensor(
                               aoT[:, PH0 + s_ * 2048 + w1_: PH0 + s_ * 2048 + dr_ * 512],
                               aoT[:, PH0 + s_ * 2048 + w1_: PH0 + s_ * 2048 + dr_ * 512],
                               diagm[:, 0:w2_], ALU.mult),
                           inc=gp, drain=False)
                    tok_ts = P.emit("scalar", [(gp, tok_mk)],
                                    lambda e, s_=slot, w1_=w1, dr_=dri, s=sa: e.activation(
                                        aoT[:, PH0 + s_ * 2048 + w1_: PH0 + s_ * 2048 + dr_ * 512],
                                        aoT[:, PH0 + s_ * 2048 + w1_: PH0 + s_ * 2048 + dr_ * 512],
                                        AF.Identity, scale=1.0,
                                        accum_out=stat[:, s + 11:s + 12]),
                                    inc=se, drain=False)
                    att_free_sa[0] = tok_e1t
                    att_free_sa[1] = dri
                    att_free_sb[0] = tok_e1
                    return (tok_ts, w1, tok_mk)

                def emit_softmax_post(i, tok_ts, w1, tok_mk):
                    slot = i % 4
                    hf = i // 8
                    dri = (i % 8) // 4 + 1
                    nv = dri + 2
                    sa = 40 + (i % 2) * 12
                    if w1 > 0:
                        tok_sl = P.emit("vector", [(se, tok_ts)],
                               lambda e, s=sa, hf_=hf: e.scalar_tensor_tensor(
                                   stat[:, s + 8:s + 9], stat[:, s + 7:s + 8], kcol[:, hf_:hf_ + 1],
                                   stat[:, s + 6:s + 7], ALU.mult, ALU.add),
                               inc=ve, drain=False)
                        tok_la = P.emit("scalar", [(ve, tok_sl)],
                                        lambda e, s=sa: e.activation(
                                            stat[:, s + 8:s + 9], stat[:, s + 8:s + 9], AF.Identity,
                                            bias=stat[:, s + 11:s + 12], scale=1.0),
                                        inc=se, drain=False)
                        tok_rc = P.emit("vector", [(se, tok_la)],
                               lambda e, s=sa: e.reciprocal(stat[:, s + 9:s + 10], stat[:, s + 8:s + 9]),
                               inc=ve, drain=False)
                    else:
                        P.emit("vector", [(se, tok_ts)],
                               lambda e, s=sa, hf_=hf: e.scalar_tensor_tensor(
                                   stat[:, s + 8:s + 9], stat[:, s + 7:s + 8], kcol[:, hf_:hf_ + 1],
                                   stat[:, s + 11:s + 12], ALU.mult, ALU.add),
                               inc=ve, drain=False)
                        tok_rc = P.emit("vector", [],
                               lambda e, s=sa: e.reciprocal(stat[:, s + 9:s + 10], stat[:, s + 8:s + 9]),
                               inc=ve)
                    tok_rl2 = P.emit("vector", [],
                           lambda e, s=sa, hf_=hf: e.tensor_scalar(
                               stat[:, s + 10:s + 11], stat[:, s + 9:s + 10], kcol[:, hf_:hf_ + 1], None, ALU.mult),
                           inc=ve)
                    P.emit("vector", [(gp, tok_mk)],
                           lambda e, s_=slot, dr_=dri, s=sa: e.tensor_scalar(
                               aoT[:, PH0 + s_ * 2048: PH0 + s_ * 2048 + dr_ * 512],
                               aoT[:, PH0 + s_ * 2048: PH0 + s_ * 2048 + dr_ * 512],
                               stat[:, s + 9:s + 10], None, ALU.mult),
                           inc=ve, drain=False)
                    tok_pn = P.emit("vector", [],
                                    lambda e, s_=slot, dr_=dri, nv_=nv, s=sa: e.tensor_scalar(
                                        aoT[:, PH0 + s_ * 2048 + dr_ * 512: PH0 + s_ * 2048 + nv_ * 512],
                                        aoT[:, PH0 + s_ * 2048 + dr_ * 512: PH0 + s_ * 2048 + nv_ * 512],
                                        stat[:, s + 10:s + 11], None, ALU.mult),
                                    inc=ve)
                    phat_ve[slot] = tok_pn

                class TavState:
                    pass

                def tav_begin(grp):
                    st = TavState()
                    st.grp = grp
                    st.tiles = (2 * grp, 2 * grp + 1)
                    st.hf = grp // 4
                    st.ob = grp % 2
                    dr2 = (st.tiles[1] % 8) // 4 + 1
                    st.klist = [(st.hf * 8 + k, k) for k in range(2 * (grp % 4) + 2)]
                    st.klist += [((1 - st.hf) * 8 + k8, dr2 * 4 + k8) for k8 in range(8)]
                    st.nk = len(st.klist)
                    st.pending = None
                    return st

                def tav_av(st, ki, kseq, ptb, ev):
                    wmm = [ev]
                    if ki == 0:
                        wmm.append((ve, o_ve[st.ob] if o_ve[st.ob] > 0 else csx_tok))
                    def mkav(e, ks=kseq, ob_=st.ob, first=(ki == 0), last=(ki == st.nk - 1), p_=ptb, vb_=vb):
                        e.ldweights(POOL_bf[:, vb_ + ks * 128: vb_ + ks * 128 + 128])
                        return e.matmul(psB[:, ob_ * 512: ob_ * 512 + 256],
                                        POOL_bf[:, vb_ + ks * 128: vb_ + ks * 128 + 128],
                                        POOL_bf[:, PT0 + p_ * 512: PT0 + p_ * 512 + 256],
                                        start=first, stop=last)
                    tok_av = P.emit("tensor", [(sm, v) for (sm, v) in wmm if v > 0], mkav, inc=pe)
                    pt_pe[ptb] = tok_av
                    return tok_av

                def tav_steps(st, k_from, k_to):
                    for ki in range(k_from, k_to):
                        kseq, kph = st.klist[ki]
                        ptb = ki % 2
                        b = tb_next[0]
                        tb_next[0] ^= 1
                        wt = []
                        if tb_ve[b] != 0:
                            wt.append(tb_ve[b])
                        for ti, i in enumerate(st.tiles):
                            slot = i % 4
                            wt2 = list(wt) + [(ve, phat_ve[slot])]
                            tok_t = P.emit("tensor", wt2,
                                           lambda e, s_=slot, kp=kph, b_=b, ti_=ti: e.transpose(
                                               psT[:, b_ * 1024 + ti_ * 128: b_ * 1024 + ti_ * 128 + 128],
                                               aoT[:, PH0 + s_ * 2048 + kp * 128: PH0 + s_ * 2048 + kp * 128 + 128],
                                               ident[:, :]),
                                           inc=pe)
                            phat_pe[slot] = tok_t
                            wt = []
                        tb_pe[b] = tok_t
                        wv = [(pe, tok_t)]
                        if pt_pe[ptb] > 0:
                            wv.append((pe, pt_pe[ptb]))
                        if ki % 2 == 0:
                            tok_ev = P.emit("scalar", wv,
                                            lambda e, b_=b, p_=ptb: e.copy(
                                                POOL_bf[:, PT0 + p_ * 512: PT0 + p_ * 512 + 256],
                                                psT[:, b_ * 1024: b_ * 1024 + 256]),
                                            inc=se, drain=False)
                            ev = (se, tok_ev)
                        else:
                            tok_ev = P.emit("vector", wv,
                                            lambda e, b_=b, p_=ptb: e.tensor_copy(
                                                POOL_bf[:, PT0 + p_ * 512: PT0 + p_ * 512 + 256],
                                                psT[:, b_ * 1024: b_ * 1024 + 256]),
                                            inc=ve, drain=False)
                            ev = (ve, tok_ev)
                        tb_ve[b] = ev
                        pt_ve[ptb] = tok_ev
                        if st.pending is not None:
                            tav_av(st, *st.pending)
                        st.pending = (ki, kseq, ptb, ev)

                def tav_end(st):
                    tok_av = tav_av(st, *st.pending)
                    st.pending = None
                    o_pe[st.ob] = tok_av
                    grp = st.grp
                    if st.hf == 0:
                        o_ve[st.ob] = P.emit("vector", [(pe, tok_av)],
                                             lambda e, ha=hloc, g_=grp, ob_=st.ob: e.tensor_copy(
                                                 aoT[:, ha * 1024 + g_ * 256: ha * 1024 + g_ * 256 + 256],
                                                 psB[:, ob_ * 512: ob_ * 512 + 256]),
                                             inc=ve, drain=False)
                    else:
                        sst = hloc % 2
                        wsv = [(pe, tok_av)]
                        if grp == 4 and snd_tok[sst] > 0:
                            wsv.append((s_snd[sst], snd_tok[sst]))
                        o_ve[st.ob] = P.emit("vector", wsv,
                                             lambda e, g_=grp, ob_=st.ob, ss=sst: e.tensor_copy(
                                                 POOL_bf[:, SSTG + ss * 1024 + (g_ - 4) * 256: SSTG + ss * 1024 + (g_ - 4) * 256 + 256],
                                                 psB[:, ob_ * 512: ob_ * 512 + 256]),
                                             inc=ve, drain=False)

                prev = None
                for grp in range(8):
                    e_a = emit_softmax_pre(2 * grp)
                    if prev is not None:
                        tav_steps(prev, 0, prev.nk // 2)
                    e_b = emit_softmax_pre(2 * grp + 1)
                    if prev is not None:
                        tav_steps(prev, prev.nk // 2, prev.nk)
                        tav_end(prev)
                    emit_softmax_post(2 * grp, *e_a)
                    emit_softmax_post(2 * grp + 1, *e_b)
                    prev = tav_begin(grp)
                tav_steps(prev, 0, prev.nk)
                tav_end(prev)

                sst = hloc % 2
                snd_tok[sst] = dma("sync", s_snd[sst], dsend[hloc],
                                   POOL_bf[:, SSTG + sst * 1024: SSTG + sst * 1024 + 1024],
                                   waits=[(ve, o_ve[0]), (ve, o_ve[1])])
            qkv_free_ve[0] = qkv_free_ve[1] = ve.n
            qkv_free_se[0] = qkv_free_se[1] = se.n
            qkv_free_gp[0] = qkv_free_gp[1] = gp.n
        att_done_ve = ve.n
        att_done_se = se.n
        att_done_pe = pe.n

        merge_tok = 0
        if KLIMIT >= 30:
            cc_tok = P.emit("gpsimd", [(s_snd[0], snd_tok[0]), (s_snd[1], snd_tok[1])],
                            lambda e: e.collective_compute(
                                "AllGather", ALU.bypass,
                                replica_groups=[[0, 1], [2, 3], [4, 5], [6, 7]],
                                ins=[dsend], outs=[dgath]),
                            inc=gp)
            r0 = dma("sync", s_rcv, aoT[:, 8192:16384],
                     dgath[0].rearrange("h p f -> p h f"),
                     waits=[(gp, cc_tok), (pe, att_done_pe), (ve, att_done_ve)])
            r1 = dma("sync", s_rcv, POOL_bf[:, 0:8192],
                     dgath[1].rearrange("h p f -> p h f"), waits=[])
            P.emit("vector", [(s_rcv, r1)],
                   lambda e: e.tensor_scalar(aoT[:, 8192:16384], aoT[:, 8192:16384],
                                             kcol[:, 0:1], None, ALU.mult), inc=ve)
            merge_tok = P.emit("vector", [],
                               lambda e: e.scalar_tensor_tensor(
                                   aoT[:, 8192:16384], POOL_bf[:, 0:8192], kcol[:, 1:2],
                                   aoT[:, 8192:16384], ALU.mult, ALU.add), inc=ve)

        WG0, WU0, WD0, WO0 = 0, 4096, 8192, 19456
        gu_tok = [0, 0]
        gu_free = [0, 0]
        wd_tok = [0, 0]
        wd_free = [0, 0]
        wo_tok = [0]
        wo_free = [0]
        xf_tok = [0, 0]
        xf_free = [0, 0]
        gps_ve = [0, 0]
        gps_se = [0, 0]
        hid_read_pe = [0]
        out_tok = [0, 0]
        silu_free = [0]

        for tb in range(2 if KLIMIT >= 40 else 0):
            for ht in range(NC_H):
                ping = ht % 2
                w = [(pe, gu_free[ping])] if gu_tok[ping] > 0 else [(pe, att_done_pe), (se, att_done_se)]
                dma("gpsimd", s_wg[ping], RS_bf[:, WG0 + ping * 2048: WG0 + ping * 2048 + 2048], dwg[ht], waits=w)
                gu_tok[ping] = dma("gpsimd", s_wg[ping], RS_bf[:, WU0 + ping * 2048: WU0 + ping * 2048 + 2048], dwu[ht])
                gcol = ping * 1024
                ucol = ping * 1024 + 512
                w = [(s_wg[ping], gu_tok[ping]), (ve, csx_tok), (ve, xnT_tok), (ve, att_done_ve)]
                if gps_ve[ping] > 0:
                    w += [(ve, gps_ve[ping]), (se, gps_se[ping])]
                for c in range(NC_D):
                    def mkg(e, c_=c, g_=gcol, p_=ping, tb_=tb):
                        e.ldweights(RS_bf[:, WG0 + p_ * 2048 + c_ * 128: WG0 + p_ * 2048 + c_ * 128 + 128])
                        return e.matmul(psA[:, g_:g_ + 512],
                                        RS_bf[:, WG0 + p_ * 2048 + c_ * 128: WG0 + p_ * 2048 + c_ * 128 + 128],
                                        xnT[:, c_ * T + tb_ * 512: c_ * T + tb_ * 512 + 512],
                                        start=(c_ == 0), stop=False)
                    P.emit("tensor", w if c == 0 else [], mkg, inc=None)
                P.emit("tensor", [], lambda e, g_=gcol, tb_=tb: e.matmul(
                    psA[:, g_:g_ + 512], ones_r[0:1, :], csx_hi[0:1, tb_ * 512: tb_ * 512 + 512], start=False, stop=False), inc=None)
                P.emit("tensor", [], lambda e, g_=gcol, tb_=tb: e.matmul(
                    psA[:, g_:g_ + 512], ones_r[0:1, :], csx_lo[0:1, tb_ * 512: tb_ * 512 + 512], start=False, stop=True), inc=None)
                for c in range(NC_D):
                    def mku(e, c_=c, u_=ucol, p_=ping, tb_=tb):
                        e.ldweights(RS_bf[:, WU0 + p_ * 2048 + c_ * 128: WU0 + p_ * 2048 + c_ * 128 + 128])
                        return e.matmul(psA[:, u_:u_ + 512],
                                        RS_bf[:, WU0 + p_ * 2048 + c_ * 128: WU0 + p_ * 2048 + c_ * 128 + 128],
                                        xnT[:, c_ * T + tb_ * 512: c_ * T + tb_ * 512 + 512],
                                        start=(c_ == 0), stop=False)
                    P.emit("tensor", [], mku, inc=None)
                P.emit("tensor", [], lambda e, u_=ucol, tb_=tb: e.matmul(
                    psA[:, u_:u_ + 512], ones_r[0:1, :], csx_hi[0:1, tb_ * 512: tb_ * 512 + 512], start=False, stop=False), inc=None)
                tok_up = P.emit("tensor", [], lambda e, u_=ucol, tb_=tb: e.matmul(
                    psA[:, u_:u_ + 512], ones_r[0:1, :], csx_lo[0:1, tb_ * 512: tb_ * 512 + 512], start=False, stop=True), inc=pe)
                gu_free[ping] = tok_up
                tok_sg = P.emit("scalar", [(pe, tok_up), (ve, silu_free[0])],
                                lambda e, g_=gcol, ht_=ht: e.activation(
                                    RS[:, SILU_G:SILU_G + 512], psA[:, g_:g_ + 512], AF.Silu, scale=sgc[:, ht_:ht_ + 1]),
                                inc=se, drain=False)
                tok_hid = P.emit("vector", [(se, tok_sg), (pe, hid_read_pe[0]), (ve, merge_tok)],
                                 lambda e, u_=ucol, ht_=ht: e.scalar_tensor_tensor(
                                     POOL_bf[:, HID0 + ht_ * 512: HID0 + ht_ * 512 + 512],
                                     psA[:, u_:u_ + 512], suc[:, ht_:ht_ + 1],
                                     RS[:, SILU_G:SILU_G + 512], ALU.mult, ALU.mult),
                                 inc=ve, drain=False)
                gps_ve[ping] = tok_hid
                gps_se[ping] = tok_sg
                silu_free[0] = tok_hid
            hid_all = ve.n

            w0 = [(ve, hid_all)]
            if o_ve[0] > 0:
                w0 += [(ve, o_ve[0]), (ve, o_ve[1])]
            for c in range(NC_H):
                P.emit("tensor", w0 if c == 0 else [],
                       lambda e, c_=c: e.matmul(psB[0:1, 0:512], ones_c[:, :],
                                                POOL_bf[:, HID0 + c_ * 512: HID0 + c_ * 512 + 512],
                                                start=(c_ == 0), stop=(c_ == NC_H - 1)),
                       inc=pe if c == NC_H - 1 else None)
            tok_csh = P.emit("vector", [(pe, pe.n)],
                             lambda e: e.tensor_copy(csh[:, :], psB[0:1, 0:512]), inc=ve)
            tok_n = P.emit("vector", [(ve, tok_csh)],
                           lambda e: e.tensor_scalar(csh[:, :], csh[:, :], -128.0, None, ALU.mult), inc=ve)
            tok_hi2 = P.emit("vector", [(ve, tok_n)],
                             lambda e: e.tensor_copy(csh_hi[:, :], csh[:, :]), inc=ve)
            csh_tok = P.emit("vector", [(ve, tok_hi2)],
                             lambda e: e.tensor_tensor(csh_lo[:, :], csh[:, :], csh_hi[:, :], ALU.subtract), inc=ve)

            for dt in range(16):
                ping = dt % 2
                w = [(pe, wd_free[ping])] if wd_tok[ping] > 0 else []
                wd_tok[ping] = dma("gpsimd", s_wd[ping], RS_bf[:, WD0 + ping * 5632: WD0 + ping * 5632 + 5632], dwd[dt], waits=w)
                w = [(pe, wo_free[0])] if wo_tok[0] > 0 else []
                wo_tok[0] = dma("gpsimd", s_wo, RS_bf[:, WO0:WO0 + 2048], dwout[dt], waits=w)
                if xf_tok[ping] > 0:
                    w = [(ve, xf_free[ping]), (s_out[ping], out_tok[ping])]
                else:
                    w = [(pe, att_done_pe)]
                xf_tok[ping] = dma("sync", s_xf[ping], RS[:, XF0 + ping * 512: XF0 + ping * 512 + 512],
                                   dxT[dt * 128: dt * 128 + 128, tb * 512: tb * 512 + 512], waits=w)

                gp_col = ping * 1024
                sp_col = ping * 1024 + 512
                wmm = [(s_wd[ping], wd_tok[ping]), (ve, csh_tok)]
                if gps_ve[ping] > 0:
                    wmm += [(ve, gps_ve[ping]), (se, gps_se[ping])]
                for c in range(NC_H):
                    def mkd(e, c_=c, g_=gp_col, p_=ping):
                        e.ldweights(RS_bf[:, WD0 + p_ * 5632 + c_ * 128: WD0 + p_ * 5632 + c_ * 128 + 128])
                        return e.matmul(psA[:, g_:g_ + 512],
                                        RS_bf[:, WD0 + p_ * 5632 + c_ * 128: WD0 + p_ * 5632 + c_ * 128 + 128],
                                        POOL_bf[:, HID0 + c_ * 512: HID0 + c_ * 512 + 512],
                                        start=(c_ == 0), stop=False)
                    P.emit("tensor", wmm if c == 0 else [], mkd, inc=None)
                P.emit("tensor", [], lambda e, g_=gp_col: e.matmul(
                    psA[:, g_:g_ + 512], ones_r[0:1, :], csh_hi[0:1, :], start=False, stop=False), inc=None)
                tok_d = P.emit("tensor", [], lambda e, g_=gp_col: e.matmul(
                    psA[:, g_:g_ + 512], ones_r[0:1, :], csh_lo[0:1, :], start=False, stop=True), inc=pe)
                hid_read_pe[0] = tok_d
                wd_free[ping] = tok_d
                for c in range(NC_D):
                    def mko(e, c_=c, s_=sp_col, tb_=tb):
                        e.ldweights(RS_bf[:, WO0 + c_ * 128: WO0 + c_ * 128 + 128])
                        return e.matmul(psA[:, s_:s_ + 512],
                                        RS_bf[:, WO0 + c_ * 128: WO0 + c_ * 128 + 128],
                                        aoT[:, c_ * 1024 + tb_ * 512: c_ * 1024 + tb_ * 512 + 512],
                                        start=(c_ == 0), stop=(c_ == NC_D - 1))
                    P.emit("tensor", [(s_wo, wo_tok[0]), (ve, att_done_ve), (ve, merge_tok)] if c == 0 else [], mko,
                           inc=pe if c == NC_D - 1 else None)
                tok_s = pe.n
                wo_free[0] = tok_s
                tok_g = P.emit("scalar", [(pe, tok_d), (ve, silu_free[0])],
                               lambda e, g_=gp_col, dt_=dt: e.activation(
                                   RS[:, SILU_G:SILU_G + 512],
                                   psA[:, g_:g_ + 512], AF.Silu, scale=sdc[:, dt_:dt_ + 1]),
                               inc=se, drain=False)
                tok_p = P.emit("vector", [(se, tok_g), (pe, tok_s)],
                               lambda e, s_=sp_col: e.tensor_tensor(
                                   RS[:, SILU_G:SILU_G + 512], psA[:, s_:s_ + 512],
                                   RS[:, SILU_G:SILU_G + 512], ALU.mult),
                               inc=ve, drain=False)
                tok_o = P.emit("vector", [(ve, tok_p), (s_xf[ping], xf_tok[ping])],
                               lambda e, p_=ping: e.tensor_tensor(
                                   RS[:, XF0 + p_ * 512: XF0 + p_ * 512 + 512],
                                   RS[:, SILU_G:SILU_G + 512],
                                   RS[:, XF0 + p_ * 512: XF0 + p_ * 512 + 512], ALU.add),
                               inc=ve)
                xf_free[ping] = tok_o
                silu_free[0] = tok_o
                gps_ve[ping] = tok_o
                gps_se[ping] = tok_g
                out_tok[ping] = dma("sync", s_out[ping], dout[dt * 128: dt * 128 + 128, tb * 512: tb * 512 + 512],
                                    RS[:, XF0 + ping * 512: XF0 + ping * 512 + 512], waits=[(ve, tok_o)])

        if DEBUG:
            tdbg = dma("sync", s_cos, d_xnT, xnT[:, :], waits=[(ve, xnT_tok)])
            tdbg = dma("sync", s_cos, d_aoT, aoT[:, :], waits=[(ve, att_done_ve)])
            P.emit("sync", [(s_cos, tdbg)], lambda e: e.nop(), inc=None)
        P.emit("sync", [(s_out[0], out_tok[0]), (s_out[1], out_tok[1])], lambda e: e.nop(), inc=None)

        with nc.Block() as block:
            @block.sync
            def _(e):
                for f in P.ops["sync"]:
                    f(e)

            @block.gpsimd
            def _(e):
                for f in P.ops["gpsimd"]:
                    f(e)

            @block.tensor
            def _(e):
                for f in P.ops["tensor"]:
                    f(e)

            @block.vector
            def _(e):
                for f in P.ops["vector"]:
                    f(e)

            @block.scalar
            def _(e):
                for f in P.ops["scalar"]:
                    f(e)

    return nc


def _host_prep(inputs):
    x = np.asarray(inputs["L0_attn_x"], dtype=np.float32)
    w_qkv = np.asarray(inputs["L0_attn_w_qkv"], dtype=np.float32)
    w_out = np.asarray(inputs["L0_attn_w_out"], dtype=np.float32)
    qg = np.asarray(inputs["L0_attn_q_gamma"], dtype=np.float32)
    kg = np.asarray(inputs["L0_attn_k_gamma"], dtype=np.float32)
    wqg = np.asarray(inputs["L0_ffn_wq_gate"])
    wqu = np.asarray(inputs["L0_ffn_wq_up"])
    wqd = np.asarray(inputs["L0_ffn_wq_down"])
    sg = np.asarray(inputs["L0_ffn_s_gate"], dtype=np.float32)
    su = np.asarray(inputs["L0_ffn_s_up"], dtype=np.float32)
    sd = np.asarray(inputs["L0_ffn_s_down"], dtype=np.float32)

    wq = w_qkv[:, 0:2048].reshape(D, NH, HD)
    wk = w_qkv[:, 2048:4096].reshape(D, NH, HD)
    wv = w_qkv[:, 4096:6144].reshape(D, NH, HD)
    wqkv_all = np.empty((8, D, 768), dtype=np.float32)
    for p in range(8):
        wqkv_all[p, :, 0:128] = wk[:, 2 * p]
        wqkv_all[p, :, 128:256] = wk[:, 2 * p + 1]
        wqkv_all[p, :, 256:384] = wv[:, 2 * p]
        wqkv_all[p, :, 384:512] = wv[:, 2 * p + 1]
        wqkv_all[p, :, 512:640] = wq[:, 2 * p]
        wqkv_all[p, :, 640:768] = wq[:, 2 * p + 1]
    def wqkv_for(g):
        wt = wqkv_all[4 * g:4 * g + 4]
        return np.ascontiguousarray(
            wt.reshape(2, 2, NC_D, 128, 768).transpose(0, 3, 1, 2, 4))
    def wout_for(g):
        blk = w_out.reshape(16, 128, D)
        order = list(range(8 * g, 8 * g + 8)) + list(range(8 * (1 - g), 8 * (1 - g) + 8))
        wp = blk[order].reshape(D, D)
        return np.ascontiguousarray(wp.reshape(16, 128, 16, 128).transpose(2, 1, 0, 3))

    def u8(a):
        return a.astype(np.int32).view(np.uint8)[..., 0::4]

    wgT = np.ascontiguousarray(u8(wqg).T)
    wuT = np.ascontiguousarray(u8(wqu).T)
    wdT = np.ascontiguousarray(u8(wqd).T)
    wg8 = np.ascontiguousarray(wgT.reshape(16, 128, NC_H, 128).transpose(2, 1, 0, 3))
    wu8 = np.ascontiguousarray(wuT.reshape(16, 128, NC_H, 128).transpose(2, 1, 0, 3))
    wd8 = np.ascontiguousarray(wdT.reshape(NC_H, 128, 16, 128).transpose(2, 1, 0, 3))

    sgc = np.ascontiguousarray(sg.reshape(NC_H, 128).T)
    suc = np.ascontiguousarray(su.reshape(NC_H, 128).T)
    sdc = np.ascontiguousarray(sd.reshape(16, 128).T)
    qgb = np.ascontiguousarray(np.broadcast_to(qg, (128, 128)))
    kgb = np.ascontiguousarray(np.broadcast_to(kg, (128, 128)))

    inv_freq = 1.0 / (10000.0 ** (np.arange(0, HD, 2, dtype=np.float64) / HD))
    tpos = np.arange(T, dtype=np.float64)
    freqs = tpos[:, None] * inv_freq[None, :]
    emb = np.concatenate([freqs, freqs], axis=1)
    cos_t = np.cos(emb)
    ssin_t = np.sin(emb)
    ssin_t[:, 0:64] = -ssin_t[:, 0:64]

    diagm = np.zeros((128, 640), dtype=np.float32)
    qi = np.arange(128)[:, None]
    ki = np.arange(128)[None, :]
    diagm[:, 0:128] = np.where(qi >= ki, 1.0, 0.0)
    diagm[:, 128:640] = 0.0
    diagm = diagm.astype(ml_dtypes.bfloat16)

    ident = np.eye(128, dtype=np.float32).astype(ml_dtypes.bfloat16)

    per_core = []
    for core in range(8):
        b, g = core // 2, core % 2
        xb = x[b]
        if g == 0:
            xp = xb
            pos = np.arange(T)
        else:
            xp = np.ascontiguousarray(np.concatenate([xb[1024:], xb[:1024]], axis=0))
            pos = np.concatenate([np.arange(1024, 2048), np.arange(1024)])
        cosp = np.ascontiguousarray(cos_t[pos].astype(np.float16).reshape(16, 128, 128).transpose(1, 0, 2))
        sinp = np.ascontiguousarray(ssin_t[pos].astype(np.float16).reshape(16, 128, 128).transpose(1, 0, 2))
        kc = np.ascontiguousarray(np.broadcast_to(
            np.array([float(g), float(1 - g)], dtype=np.float32), (128, 2)))
        xT_own = np.ascontiguousarray(xp[:1024].T)
        per_core.append(dict(
            x=np.ascontiguousarray(xp), xT=xT_own,
            wqkv=wqkv_for(g), wout=wout_for(g), wg8=wg8, wu8=wu8, wd8=wd8,
            sgc=sgc, suc=suc, sdc=sdc, qgb=qgb, kgb=kgb,
            cosq=cosp, ssin=sinp, diagm=diagm, ident=ident, kcol=kc,
        ))
    return per_core


def kernel(**inputs):
    per_core = _host_prep(inputs)
    nc = build_program()
    res = run_bass_kernel_spmd(nc, per_core, core_ids=list(range(8)))
    out = np.empty((4, 2048, 2048), dtype=np.float32)
    for core in range(8):
        b, g = core // 2, core % 2
        shard = res.results[core]["out"]
        out[b, g * 1024:(g + 1) * 1024, :] = np.asarray(shard).T
    return out


# version 22
# speedup vs baseline: 1.1295x; 1.1295x over previous
import os
import numpy as np
import ml_dtypes
import contextlib
import concourse.bass as bass
import concourse.mybir as mybir
from concourse.bass_utils import run_bass_kernel_spmd

F32 = mybir.dt.float32
F16 = mybir.dt.float16
BF16 = mybir.dt.bfloat16
U8 = mybir.dt.uint8
AF = mybir.ActivationFunctionType
ALU = mybir.AluOpType
AX = mybir.AxisListType

T = 2048
D = 2048
NH = 16
HD = 128
NC_D = 16
NC_H = 44
OWN = 1024
NEG = -1e30
C_ATT = float(1.0 / np.sqrt(128.0))
NPAIR = 4
KLIMIT = int(os.environ.get("KLIMIT", "99"))
KPAIRS = int(os.environ.get("KPAIRS", str(NPAIR)))
KSUB = int(os.environ.get("KSUB", "9"))


class Sem:
    def __init__(self, h):
        self.h = h
        self.n = 0


class Plan:
    ENGS = ("sync", "gpsimd", "tensor", "vector", "scalar")

    def __init__(self):
        self.ops = {e: [] for e in self.ENGS}

    DRAIN = frozenset(("vector", "scalar", "gpsimd"))

    def emit(self, eng, waits, fn, inc=None, inc_by=1, drain=None):
        tok = None
        if inc is not None:
            inc.n += inc_by
            tok = inc.n
        if drain is None:
            drain = eng in self.DRAIN
        def op(e, waits=tuple(waits), fn=fn, inc=inc, inc_by=inc_by, drain=drain):
            for s, v in waits:
                if v > 0:
                    e.wait_ge(s.h, v)
            if drain:
                e.drain()
            ins = fn(e)
            if inc is not None:
                ins.then_inc(inc.h, inc_by)
        self.ops[eng].append(op)
        return tok


def build_program():
    nc = bass.Bass("TRN2", target_bir_lowering=False)

    dx = nc.dram_tensor("x", [T, D], F32, kind="ExternalInput").ap()
    dxT = nc.dram_tensor("xT", [D, OWN], F32, kind="ExternalInput").ap()
    dwqkv = nc.dram_tensor("wqkv", [NPAIR // 2, 128, 2, NC_D, 768], F32, kind="ExternalInput").ap()
    dwout = nc.dram_tensor("wout", [16, 128, NC_D, 128], F32, kind="ExternalInput").ap()
    dwg = nc.dram_tensor("wg8", [NC_H, 128, NC_D, 128], U8, kind="ExternalInput").ap()
    dwu = nc.dram_tensor("wu8", [NC_H, 128, NC_D, 128], U8, kind="ExternalInput").ap()
    dwd = nc.dram_tensor("wd8", [16, 128, NC_H, 128], U8, kind="ExternalInput").ap()
    dsg = nc.dram_tensor("sgc", [128, NC_H], F32, kind="ExternalInput").ap()
    dsu = nc.dram_tensor("suc", [128, NC_H], F32, kind="ExternalInput").ap()
    dsd = nc.dram_tensor("sdc", [128, 16], F32, kind="ExternalInput").ap()
    dqg = nc.dram_tensor("qgb", [128, 128], F32, kind="ExternalInput").ap()
    dkg = nc.dram_tensor("kgb", [128, 128], F32, kind="ExternalInput").ap()
    dcos = nc.dram_tensor("cosq", [128, 16, 128], F16, kind="ExternalInput").ap()
    dsin = nc.dram_tensor("ssin", [128, 16, 128], F16, kind="ExternalInput").ap()
    dkcol = nc.dram_tensor("kcol", [128, 2], F32, kind="ExternalInput").ap()
    ddiag = nc.dram_tensor("diagm", [128, 640], BF16, kind="ExternalInput").ap()
    dident = nc.dram_tensor("ident", [128, 128], BF16, kind="ExternalInput").ap()
    dout = nc.dram_tensor("out", [D, OWN], F32, kind="ExternalOutput").ap()
    dsend = nc.dram_tensor("sendb", [8, 128, 1024], BF16).ap()
    dgath = nc.dram_tensor("gathb", [2, 8, 128, 1024], BF16).ap()
    DEBUG = False
    if DEBUG:
        d_xnT = nc.dram_tensor("dbg_xnT", [128, NC_D * T], BF16, kind="ExternalOutput").ap()
        d_aoT = nc.dram_tensor("dbg_aoT", [128, NH * OWN], BF16, kind="ExternalOutput").ap()
    else:
        d_xnT = d_aoT = None

    P = Plan()
    with contextlib.ExitStack() as st:
        E = st.enter_context

        xnT = E(nc.sbuf_tensor("xnT", [128, NC_D * T], BF16))
        RS = E(nc.sbuf_tensor("RS", [128, 12416], F32))
        aoT = E(nc.sbuf_tensor("aoT", [128, NH * OWN], BF16))
        POOL = E(nc.sbuf_tensor("POOL", [128, 11264], F32))

        qgb = E(nc.sbuf_tensor("s_qgb", [128, 128], BF16))
        kgb = E(nc.sbuf_tensor("s_kgb", [128, 128], BF16))
        cosq = E(nc.sbuf_tensor("s_cosq", [128, 16 * 128], F16))
        ssin = E(nc.sbuf_tensor("s_ssin", [128, 16 * 128], F16))
        diagm = E(nc.sbuf_tensor("s_diagm", [128, 640], BF16))
        ident = E(nc.sbuf_tensor("s_ident", [128, 128], BF16))
        kcol = E(nc.sbuf_tensor("s_kcol", [128, 2], F32))
        ones_c = E(nc.sbuf_tensor("ones_c", [128, 1], BF16))
        ones_r = E(nc.sbuf_tensor("ones_r", [1, 128], BF16))
        sgc = E(nc.sbuf_tensor("s_sgc", [128, NC_H], F32))
        suc = E(nc.sbuf_tensor("s_suc", [128, NC_H], F32))
        sdc = E(nc.sbuf_tensor("s_sdc", [128, 16], F32))
        stat = E(nc.sbuf_tensor("stat", [128, 64], F32))
        csx_hi = E(nc.sbuf_tensor("csx_hi", [1, 1024], BF16))
        csx_lo = E(nc.sbuf_tensor("csx_lo", [1, 1024], BF16))
        csh_hi = E(nc.sbuf_tensor("csh_hi", [1, 512], BF16))
        csh_lo = E(nc.sbuf_tensor("csh_lo", [1, 512], BF16))

        psA = E(nc.psum_tensor("psA", [128, 2048], F32))
        psB = E(nc.psum_tensor("psB", [128, 1024], F32))
        psT = E(nc.psum_tensor("psT", [128, 2048], BF16))

        pe = Sem(E(nc.semaphore("pe")))
        ve = Sem(E(nc.semaphore("ve")))
        se = Sem(E(nc.semaphore("se")))
        gp = Sem(E(nc.semaphore("gp")))
        s_cst = Sem(E(nc.semaphore("s_cst")))
        s_cos = Sem(E(nc.semaphore("s_cos")))
        s_x = [Sem(E(nc.semaphore("s_x0"))), Sem(E(nc.semaphore("s_x1")))]
        s_slab = Sem(E(nc.semaphore("s_slab")))
        s_wg = [Sem(E(nc.semaphore("s_wg0"))), Sem(E(nc.semaphore("s_wg1")))]
        s_wd = [Sem(E(nc.semaphore("s_wd0"))), Sem(E(nc.semaphore("s_wd1")))]
        s_wo = Sem(E(nc.semaphore("s_wo")))
        s_xf = [Sem(E(nc.semaphore("s_xf0"))), Sem(E(nc.semaphore("s_xf1")))]
        s_out = [Sem(E(nc.semaphore("s_out0"))), Sem(E(nc.semaphore("s_out1")))]
        s_snd = [Sem(E(nc.semaphore("s_snd0"))), Sem(E(nc.semaphore("s_snd1")))]
        s_rcv = Sem(E(nc.semaphore("s_rcv")))

        RS_bf = RS.ap().bitcast(BF16)
        POOL_bf = POOL.ap().bitcast(BF16)

        KP, VP, QP = 0, 4096, 8192
        SSTG = 12288
        PT0 = 14336
        SCRP = [7680, 8448]
        KAP = [9216, 9728]
        QAP = [10240, 10752]
        HID0 = 0
        PH0 = 8192

        def bc2(tbl, t, lo, n):
            return bass.AP(tbl.ap().tensor, t * 128 + lo, [[2048, 128], [0, 2], [1, n]])
        XF0 = 10880
        SILU_G = 11904
        csx = RS[0:1, XF0:XF0 + 1024]
        csh = RS[0:1, SILU_G:SILU_G + 512]

        def dma(eng, sem, out_ap, in_ap, waits=()):
            w = list(waits)
            if sem.n > 0:
                w.append((sem, sem.n))
            return P.emit(eng, w,
                          lambda e, o=out_ap, i=in_ap: e.dma_start(out=o, in_=i),
                          inc=sem, inc_by=16, drain=False)

        tb_pe = [0, 0]
        tb_ve = [0, 0]
        tb_next = [0]

        def transpose_to(src_bf_ap_fn, dst_fn, waits, width=128, evac="vector"):
            b = tb_next[0]
            tb_next[0] ^= 1
            wt = list(waits)
            es = ve if evac == "vector" else se
            if tb_ve[b] != 0:
                wt.append(tb_ve[b])
            tok_t = P.emit("tensor", wt,
                           lambda e, b_=b: e.transpose(psT[:, b_ * 1024: b_ * 1024 + width],
                                                       src_bf_ap_fn(), ident[:, :]),
                           inc=pe)
            tb_pe[b] = tok_t
            if evac == "vector":
                tok_v = P.emit("vector", [(pe, tok_t)],
                               lambda e, b_=b: e.tensor_copy(dst_fn(), psT[:, b_ * 1024: b_ * 1024 + width]),
                               inc=ve, drain=False)
            else:
                tok_v = P.emit("scalar", [(pe, tok_t)],
                               lambda e, b_=b: e.copy(dst_fn(), psT[:, b_ * 1024: b_ * 1024 + width]),
                               inc=se, drain=False)
            tb_ve[b] = (es, tok_v)
            return tok_t, tok_v

        t_const = 0
        for o, i in ((qgb[:, :], dqg), (kgb[:, :], dkg), (diagm[:, :], ddiag), (kcol[:, :], dkcol),
                     (ident[:, :], dident),
                     (sgc[:, :], dsg), (suc[:, :], dsu), (sdc[:, :], dsd)):
            t_const = dma("gpsimd", s_cst, o, i)
        t_cos = dma("sync", s_cos, cosq[:, :], dcos)
        t_cos = dma("sync", s_cos, ssin[:, :], dsin)
        P.emit("gpsimd", [], lambda e: e.memset(ones_c[:, :], 1.0), inc=gp)
        t_ones = P.emit("gpsimd", [], lambda e: e.memset(ones_r[:, :], 1.0), inc=gp)
        for eng in ("tensor", "vector", "scalar"):
            P.emit(eng, [(s_cst, t_const), (gp, t_ones), (s_cos, t_cos)], lambda e: e.nop())

        ln_apply_tok = [0, 0]
        ln_sm_tok = [0, 0]
        ln_sqr_read = [0, 0]
        slab_pe = [0, 0]

        for t in range(16):
            pg = t % 2
            xc = pg * 2048
            sqc = 4096 + pg * 2048
            sb = pg * 10
            w = []
            if t >= 2:
                w = [(se, ln_apply_tok[pg]), (ve, ln_sm_tok[pg])]
            tok_x = dma("sync", s_x[pg], RS[:, xc:xc + 2048],
                        dx.rearrange("(n p) f -> p n f", p=128)[:, t, :], waits=w)
            tok_sq = P.emit("scalar", [(s_x[pg], tok_x)],
                            lambda e, a=sqc, bcol=xc: e.activation(RS[:, a:a + 2048], RS[:, bcol:bcol + 2048], AF.Square),
                            inc=se, drain=False)
            ln_sm_tok[pg] = P.emit("vector", [(s_x[pg], tok_x)],
                                   lambda e, bcol=xc, s=sb: e.reduce_sum(stat[:, s:s + 1], RS[:, bcol:bcol + 2048], axis=AX.X),
                                   inc=ve)
            P.emit("vector", [(se, tok_sq)],
                   lambda e, a=sqc, s=sb: e.reduce_sum(stat[:, s + 1:s + 2], RS[:, a:a + 2048], axis=AX.X), inc=ve)
            P.emit("vector", [], lambda e, s=sb: e.tensor_scalar(stat[:, s + 2:s + 3], stat[:, s:s + 1], 1.0 / 2048.0, None, ALU.mult), inc=ve)
            P.emit("vector", [], lambda e, s=sb: e.tensor_scalar(stat[:, s + 3:s + 4], stat[:, s + 1:s + 2], 1.0 / 2048.0, None, ALU.mult), inc=ve)
            P.emit("vector", [], lambda e, s=sb: e.tensor_tensor(stat[:, s + 4:s + 5], stat[:, s + 2:s + 3], stat[:, s + 2:s + 3], ALU.mult), inc=ve)
            P.emit("vector", [], lambda e, s=sb: e.tensor_tensor(stat[:, s + 5:s + 6], stat[:, s + 3:s + 4], stat[:, s + 4:s + 5], ALU.subtract), inc=ve)
            tok_v = P.emit("vector", [], lambda e, s=sb: e.tensor_scalar(stat[:, s + 6:s + 7], stat[:, s + 5:s + 6], 1e-5, None, ALU.add), inc=ve)
            wv = [(ve, tok_v)]
            if t >= 2:
                wv.append((ve, ln_sqr_read[pg]))
            tok_sqr = P.emit("scalar", wv, lambda e, s=sb: e.activation(stat[:, s + 7:s + 8], stat[:, s + 6:s + 7], AF.Sqrt), inc=se)
            tok_rs = P.emit("vector", [(se, tok_sqr)], lambda e, s=sb: e.reciprocal(stat[:, s + 8:s + 9], stat[:, s + 7:s + 8]), inc=ve)
            ln_sqr_read[pg] = tok_rs
            tok_bias = P.emit("vector", [],
                              lambda e, s=sb: e.tensor_scalar(stat[:, s + 9:s + 10], stat[:, s + 2:s + 3], stat[:, s + 8:s + 9], -1.0, ALU.mult, ALU.mult),
                              inc=ve)
            xcol = pg * 2048
            w = [(ve, tok_bias)]
            if t >= 2:
                w.append((pe, slab_pe[pg]))
            ln_apply_tok[pg] = P.emit(
                "scalar", w,
                lambda e, bcol=xc, xo=xcol, s=sb: e.activation(POOL_bf[:, xo:xo + 2048], RS[:, bcol:bcol + 2048], AF.Identity,
                                                               bias=stat[:, s + 9:s + 10], scale=stat[:, s + 8:s + 9]),
                inc=se)
            for c in range(NC_D):
                tok_t, _ = transpose_to(
                    (lambda xo=xcol, c_=c: POOL_bf[:, xo + c_ * 128: xo + c_ * 128 + 128]),
                    (lambda c_=c, t_=t: xnT[:, c_ * T + t_ * 128: c_ * T + t_ * 128 + 128]),
                    [(se, ln_apply_tok[pg])])
            slab_pe[pg] = tok_t
        xnT_tok = max((x[1] if x != 0 else 0) for x in tb_ve)

        for tb in range(2 if KLIMIT >= 2 else 0):
            for c in range(NC_D):
                P.emit("tensor", [(ve, xnT_tok), (gp, t_ones)] if c == 0 else [],
                       lambda e, c_=c, tb_=tb: e.matmul(
                           psB[0:1, tb_ * 512: tb_ * 512 + 512], ones_c[:, :],
                           xnT[:, c_ * T + tb_ * 512: c_ * T + tb_ * 512 + 512],
                           start=(c_ == 0), stop=(c_ == NC_D - 1)),
                       inc=pe if c == NC_D - 1 else None)
            P.emit("vector", [(pe, pe.n)],
                   lambda e, tb_=tb: e.tensor_copy(csx[:, tb_ * 512: tb_ * 512 + 512], psB[0:1, tb_ * 512: tb_ * 512 + 512]),
                   inc=ve)
        tok_n = P.emit("vector", [], lambda e: e.tensor_scalar(csx[:, :], csx[:, :], -128.0, None, ALU.mult), inc=ve)
        tok_hi = P.emit("vector", [(ve, tok_n)], lambda e: e.tensor_copy(csx_hi[:, :], csx[:, :]), inc=ve)
        csx_tok = P.emit("vector", [(ve, tok_hi)], lambda e: e.tensor_tensor(csx_lo[:, :], csx[:, :], csx_hi[:, :], ALU.subtract), inc=ve)
        csum_pe = pe.n

        WSLAB = 24576
        wslab_tok = [0]
        wslab_free_pe = [0]

        qkv_free_ve = [0, 0]
        qkv_free_se = [0, 0]
        qkv_free_gp = [0, 0]

        rmsK_gp = [0, 0]
        rmsK_ve = [0, 0]
        rmsQ_ve = [0, 0]
        khat_gp = [0, 0]
        att_free_se = 0
        att_free_ve = 0
        att_free_sa = [0, 0]
        att_free_sb = [0]
        phat_pe = [0, 0, 0, 0]
        phat_ve = [0, 0, 0, 0]
        o_pe = [0, 0]
        o_ve = [0, 0]
        pt_pe = [0, 0]
        pt_ve = [0, 0]
        rms_it = [0]
        scr_ve = [0, 0]
        scr_gp = [0, 0]
        kh_pe = [0]
        qh_pe = [0]
        snd_tok = [0, 0]

        att_done_ve = 0

        def qkv_evac(t, pcol, tok_mm):
            SCR = SCRP[t % 2]
            KA = KAP[t % 2]
            QA = QAP[t % 2]
            KH = SCR * 2
            QH = SCR * 2 + 1024
            tok_kh = 0
            tok_qh = 0
            wcp = [(pe, tok_mm)]
            if scr_ve[t % 2] > 0:
                wcp += [(ve, scr_ve[t % 2]), (gp, scr_gp[t % 2])]
            tok_cp = P.emit("scalar", wcp,
                            lambda e, pc=pcol, S=SCR: e.copy(POOL[:, S:S + 768], psA[:, pc:pc + 768]),
                            inc=se, drain=False)
            if KSUB >= 3:
                sqk = 16 + (t % 2) * 16
                P.emit("vector", [(se, tok_cp)],
                       lambda e, t_=t: e.tensor_copy(
                           POOL_bf[:, VP:VP + 4096].rearrange("p (h f) -> p h f", h=2)[:, :, t_ * 128:t_ * 128 + 128],
                           POOL[:, SCR + 256:SCR + 512].rearrange("p (h f) -> p h f", h=2)),
                       inc=ve, drain=False)
                if KSUB >= 4:
                    nsl = 4
                    wk = [(gp, scr_gp[0])] if scr_gp[0] > 0 else []
                    P.emit("vector", wk,
                           lambda e: e.tensor_tensor(POOL[:, KA:KA + 256], POOL[:, SCR:SCR + 256],
                                                     POOL[:, SCR:SCR + 256], ALU.mult), inc=ve, drain=False)
                    P.emit("vector", [],
                           lambda e, s=sqk: e.reduce_sum(stat[:, s:s + 2],
                                                         POOL[:, KA:KA + 256].rearrange("p (h f) -> p h f", h=2),
                                                         axis=AX.X), inc=ve)
                    if True:
                        P.emit("vector", [],
                               lambda e: e.tensor_tensor(POOL[:, QA:QA + 256], POOL[:, SCR + 512:SCR + 768],
                                                         POOL[:, SCR + 512:SCR + 768], ALU.mult), inc=ve, drain=False)
                        P.emit("vector", [],
                               lambda e, s=sqk: e.reduce_sum(stat[:, s + 2:s + 4],
                                                             POOL[:, QA:QA + 256].rearrange("p (h f) -> p h f", h=2),
                                                             axis=AX.X), inc=ve)
                    P.emit("vector", [],
                           lambda e, s=sqk: e.tensor_scalar(
                               stat[:, s + 4:s + 6], stat[:, s:s + 2], 1.0 / 128.0, 1e-6,
                               ALU.mult, ALU.add), inc=ve)
                    tok_st = P.emit("vector", [],
                                    lambda e, s=sqk: e.tensor_scalar(
                                        stat[:, s + 6:s + 8], stat[:, s + 2:s + 4], 1.0, 128e-6,
                                        ALU.mult, ALU.add), inc=ve, drain=False)
                    tok_sqr = P.emit("scalar", [(ve, tok_st)],
                                     lambda e, s=sqk, n=nsl: e.activation(stat[:, s + 8:s + 8 + n],
                                                                          stat[:, s + 4:s + 4 + n], AF.Sqrt),
                                     inc=se, drain=False)
                    tok_rc = P.emit("vector", [(se, tok_sqr)],
                                    lambda e, s=sqk, n=nsl: e.reciprocal(stat[:, s + 12:s + 12 + n],
                                                                         stat[:, s + 8:s + 8 + n]), inc=ve)
                    for h in (0, 1):
                        P.emit("vector", [],
                               lambda e, h_=h, s=sqk: e.scalar_tensor_tensor(
                                   POOL[:, SCR + h_ * 128: SCR + h_ * 128 + 128],
                                   POOL[:, SCR + h_ * 128: SCR + h_ * 128 + 128],
                                   stat[:, s + 12 + h_:s + 13 + h_], kgb[:, :], ALU.mult, ALU.mult),
                               inc=ve, drain=(h == 0))
                        if True:
                            P.emit("vector", [],
                                   lambda e, h_=h, s=sqk: e.scalar_tensor_tensor(
                                       POOL[:, SCR + 512 + h_ * 128: SCR + 512 + h_ * 128 + 128],
                                       POOL[:, SCR + 512 + h_ * 128: SCR + 512 + h_ * 128 + 128],
                                       stat[:, s + 14 + h_:s + 15 + h_], qgb[:, :], ALU.mult, ALU.mult),
                                   inc=ve, drain=False)
                    tok_g1 = ve.n
                    if KSUB >= 5:
                        for h in (0, 1):
                            pass
                        wk = [(ve, tok_g1)]
                        if kh_pe[0] > 0:
                            wk.append((pe, kh_pe[0]))
                        P.emit("gpsimd", wk,
                               lambda e, t_=t: e.tensor_tensor(
                                   POOL[:, KA:KA + 256].rearrange("p (h f) -> p h f", h=2),
                                   POOL[:, SCR:SCR + 256].rearrange("p (h f) -> p h f", h=2),
                                   bc2(cosq, t_, 0, 128), ALU.mult), inc=gp, drain=False)
                        P.emit("gpsimd", [],
                               lambda e, t_=t: e.tensor_tensor(
                                   POOL[:, KA + 256:KA + 512].rearrange("p (h f) -> p h f", h=2)[:, :, 0:64],
                                   POOL[:, SCR:SCR + 256].rearrange("p (h f) -> p h f", h=2)[:, :, 64:128],
                                   bc2(ssin, t_, 0, 64),
                                   ALU.mult), inc=gp, drain=False)
                        P.emit("gpsimd", [],
                               lambda e, t_=t: e.tensor_tensor(
                                   POOL[:, KA + 256:KA + 512].rearrange("p (h f) -> p h f", h=2)[:, :, 64:128],
                                   POOL[:, SCR:SCR + 256].rearrange("p (h f) -> p h f", h=2)[:, :, 0:64],
                                   bc2(ssin, t_, 64, 64),
                                   ALU.mult), inc=gp, drain=False)
                        tok_kh = P.emit("gpsimd", [(pe, kh_pe[0])] if kh_pe[0] > 0 else [],
                                        lambda e: e.tensor_tensor(
                                            POOL_bf[:, KH:KH + 256],
                                            POOL[:, KA:KA + 256], POOL[:, KA + 256:KA + 512], ALU.add),
                                        inc=gp)
                        khat_gp[0] = tok_kh
                        if KSUB >= 7:
                            P.emit("vector", [],
                                   lambda e, t_=t: e.tensor_tensor(
                                       POOL[:, QA:QA + 256].rearrange("p (h f) -> p h f", h=2),
                                       POOL[:, SCR + 512:SCR + 768].rearrange("p (h f) -> p h f", h=2),
                                       bc2(cosq, t_, 0, 128), ALU.mult), inc=ve)
                            P.emit("vector", [],
                                   lambda e, t_=t: e.tensor_tensor(
                                       POOL[:, QA + 256:QA + 512].rearrange("p (h f) -> p h f", h=2)[:, :, 0:64],
                                       POOL[:, SCR + 512:SCR + 768].rearrange("p (h f) -> p h f", h=2)[:, :, 64:128],
                                       bc2(ssin, t_, 0, 64),
                                       ALU.mult), inc=ve, drain=False)
                            P.emit("vector", [],
                                   lambda e, t_=t: e.tensor_tensor(
                                       POOL[:, QA + 256:QA + 512].rearrange("p (h f) -> p h f", h=2)[:, :, 64:128],
                                       POOL[:, SCR + 512:SCR + 768].rearrange("p (h f) -> p h f", h=2)[:, :, 0:64],
                                       bc2(ssin, t_, 64, 64),
                                       ALU.mult), inc=ve, drain=False)
                            tok_qh = P.emit("vector", [(pe, qh_pe[0])] if qh_pe[0] > 0 else [],
                                            lambda e: e.tensor_tensor(
                                                POOL_bf[:, QH:QH + 256],
                                                POOL[:, QA:QA + 256], POOL[:, QA + 256:QA + 512], ALU.add),
                                            inc=ve)
                        scr_ve[t % 2] = ve.n
            scr_gp[t % 2] = gp.n

            def flush(t_=t, KH_=KH, QH_=QH, tok_kh_=tok_kh if KSUB >= 5 else 0,
                      tok_qh_=tok_qh if KSUB >= 7 else 0):
                if KSUB >= 6:
                    for h in (0, 1):
                        tk_t, _ = transpose_to(
                            (lambda h_=h: POOL_bf[:, KH_ + h_ * 128:KH_ + h_ * 128 + 128]),
                            (lambda h_=h: POOL_bf[:, KP + h_ * 2048 + t_ * 128: KP + h_ * 2048 + t_ * 128 + 128]),
                            [(gp, tok_kh_)], evac="scalar")
                        kh_pe[0] = tk_t
                if KSUB >= 7:
                    for h in (0, 1):
                        tq_t, _ = transpose_to(
                            (lambda h_=h: POOL_bf[:, QH_ + h_ * 128:QH_ + h_ * 128 + 128]),
                            (lambda h_=h: POOL_bf[:, QP + h_ * 2048 + t_ * 128: QP + h_ * 2048 + t_ * 128 + 128]),
                            [(ve, tok_qh_)], evac="scalar")
                        qh_pe[0] = tq_t
                scr_ve[t_ % 2] = ve.n
            return tok_cp, flush

        for pair in range(KPAIRS if KLIMIT >= 25 else 0):
            pgrp, pl = pair // 2, pair % 2
            if pl == 0:
                w = [(pe, wslab_free_pe[0])] if pair > 0 else [(se, ln_apply_tok[0]), (se, ln_apply_tok[1]), (ve, ln_sm_tok[0]), (ve, ln_sm_tok[1]), (ve, csx_tok)]
                wslab_tok[0] = dma("gpsimd", s_slab, RS_bf[:, 0:WSLAB],
                                   dwqkv[pgrp], waits=w)
            wb = pl * 12288

            pendT = []
            for t in range(16 if KSUB >= 2 else 0):
                ping = t % 2
                pcol = ping * 1024
                w = [(s_slab, wslab_tok[0]), (ve, xnT_tok)]
                if qkv_free_se[ping] > 0:
                    w.append((se, qkv_free_se[ping]))
                for c in range(NC_D):
                    def mk1(e, c_=c, pc=pcol, wb_=wb, t_=t):
                        e.ldweights(xnT[:, c_ * T + t_ * 128: c_ * T + t_ * 128 + 128])
                        return e.matmul(psA[:, pc:pc + 512],
                                        xnT[:, c_ * T + t_ * 128: c_ * T + t_ * 128 + 128],
                                        RS_bf[:, wb_ + c_ * 768: wb_ + c_ * 768 + 512],
                                        start=(c_ == 0), stop=(c_ == NC_D - 1))
                    P.emit("tensor", w if c == 0 else [], mk1, inc=None)
                    if True:
                        def mk2b(e, c_=c, pc=pcol, wb_=wb, t_=t):
                            return e.matmul(psA[:, pc + 512:pc + 768],
                                            xnT[:, c_ * T + t_ * 128: c_ * T + t_ * 128 + 128],
                                            RS_bf[:, wb_ + c_ * 768 + 512: wb_ + c_ * 768 + 768],
                                            start=(c_ == 0), stop=(c_ == NC_D - 1))
                        P.emit("tensor", [], mk2b,
                               inc=pe if c == NC_D - 1 else None)
                tok_mm = pe.n
                if pl == 1 and t == 15:
                    wslab_free_pe[0] = tok_mm

                if pendT:
                    pendT.pop(0)()
                tok_cp, tok_fl = qkv_evac(t, pcol, tok_mm)
                pendT.append(tok_fl)
                qkv_free_se[ping] = tok_cp
            while pendT:
                pendT.pop(0)()
            qkv_done_ve = ve.n
            qkv_done_se = se.n

            for h in ((0, 1) if KLIMIT >= 30 else ()):
                qb = QP + h * 2048
                kb = KP + h * 2048
                vb = VP + h * 2048
                hloc = pair * 2 + h

                def emit_softmax_pre(i):
                    nonlocal att_free_ve
                    slot = i % 4
                    hf = i // 8
                    dri = (i % 8) // 4 + 1
                    nv = dri + 2
                    sub = i % 4
                    w1 = (dri - 1) * 512 + sub * 128
                    w2 = dri * 512 - w1
                    sa = 40 + (i % 2) * 12
                    w = [(ve, qkv_done_ve), (se, qkv_done_se)]
                    if att_free_ve > 0:
                        w.append((ve, att_free_ve))
                    for v in range(nv):
                        if v < dri:
                            kcolv = (hf * 2 + v) * 512
                        else:
                            kcolv = ((1 - hf) * 2 + (v - dri)) * 512
                        def mks(e, i_=i, v_=v, kc=kcolv, qb_=qb, kb_=kb):
                            if v_ == 0:
                                e.ldweights(POOL_bf[:, qb_ + i_ * 128: qb_ + i_ * 128 + 128])
                            return e.matmul(psA[:, v_ * 512: v_ * 512 + 512],
                                            POOL_bf[:, qb_ + i_ * 128: qb_ + i_ * 128 + 128],
                                            POOL_bf[:, kb_ + kc: kb_ + kc + 512],
                                            start=True, stop=True)
                        wv_ = list(w) if v == 0 else []
                        if att_free_sa[0] > 0 and v <= att_free_sa[1]:
                            wv_.append((se, att_free_sa[0] if v < att_free_sa[1] else att_free_sb[0]))
                        elif att_free_sb[0] > 0 and v > att_free_sa[1]:
                            wv_.append((se, att_free_sb[0]))
                        tok_mmv = P.emit("tensor", wv_, mks,
                                         inc=pe if v == nv - 1 else None)
                    tok_mx = P.emit("vector", [(pe, tok_mmv)],
                                    lambda e, s=sa, n_=nv: e.reduce_max(
                                        stat[:, s + 4:s + 5], psA[:, 0:n_ * 512],
                                        axis=AX.X, negate=True),
                                    inc=ve, drain=False)
                    att_free_ve = tok_mx
                    wslot = [(ve, tok_mx)]
                    if phat_pe[slot] > 0:
                        wslot.append((pe, phat_pe[slot]))
                    tok_e1a = 0
                    if w1 > 0:
                        tok_e1a = P.emit("scalar", wslot,
                               lambda e, s_=slot, w1_=w1, s=sa: e.activation(
                                   aoT[:, PH0 + s_ * 2048: PH0 + s_ * 2048 + w1_],
                                   psA[:, 0:w1_], AF.Exp, bias=stat[:, s + 4:s + 5], scale=1.0,
                                   accum_out=stat[:, s + 6:s + 7]),
                               inc=se, drain=False)
                    tok_e1t = P.emit("scalar", wslot if w1 == 0 else [],
                                     lambda e, s_=slot, w1_=w1, dr_=dri, s=sa: e.activation(
                                         aoT[:, PH0 + s_ * 2048 + w1_: PH0 + s_ * 2048 + dr_ * 512],
                                         psA[:, w1_:dr_ * 512], AF.Exp, bias=stat[:, s + 4:s + 5], scale=1.0),
                                     inc=se, drain=False)
                    tok_e1 = P.emit("scalar", [],
                                    lambda e, s_=slot, dr_=dri, nv_=nv, s=sa: e.activation(
                                        aoT[:, PH0 + s_ * 2048 + dr_ * 512: PH0 + s_ * 2048 + nv_ * 512],
                                        psA[:, dr_ * 512:nv_ * 512], AF.Exp, bias=stat[:, s + 4:s + 5], scale=1.0,
                                        accum_out=stat[:, s + 7:s + 8]),
                                    inc=se, drain=False)
                    tok_mk = P.emit("gpsimd", [(se, tok_e1t)],
                           lambda e, s_=slot, w1_=w1, w2_=w2, dr_=dri: e.tensor_tensor(
                               aoT[:, PH0 + s_ * 2048 + w1_: PH0 + s_ * 2048 + dr_ * 512],
                               aoT[:, PH0 + s_ * 2048 + w1_: PH0 + s_ * 2048 + dr_ * 512],
                               diagm[:, 0:w2_], ALU.mult),
                           inc=gp, drain=False)
                    tok_ts = P.emit("scalar", [(gp, tok_mk)],
                                    lambda e, s_=slot, w1_=w1, dr_=dri, s=sa: e.activation(
                                        aoT[:, PH0 + s_ * 2048 + w1_: PH0 + s_ * 2048 + dr_ * 512],
                                        aoT[:, PH0 + s_ * 2048 + w1_: PH0 + s_ * 2048 + dr_ * 512],
                                        AF.Identity, scale=1.0,
                                        accum_out=stat[:, s + 11:s + 12]),
                                    inc=se, drain=False)
                    att_free_sa[0] = tok_e1t
                    att_free_sa[1] = dri
                    att_free_sb[0] = tok_e1
                    return (tok_ts, w1, tok_mk)

                def emit_softmax_post(i, tok_ts, w1, tok_mk):
                    slot = i % 4
                    hf = i // 8
                    dri = (i % 8) // 4 + 1
                    nv = dri + 2
                    sa = 40 + (i % 2) * 12
                    if w1 > 0:
                        tok_sl = P.emit("vector", [(se, tok_ts)],
                               lambda e, s=sa, hf_=hf: e.scalar_tensor_tensor(
                                   stat[:, s + 8:s + 9], stat[:, s + 7:s + 8], kcol[:, hf_:hf_ + 1],
                                   stat[:, s + 6:s + 7], ALU.mult, ALU.add),
                               inc=ve, drain=False)
                        tok_la = P.emit("scalar", [(ve, tok_sl)],
                                        lambda e, s=sa: e.activation(
                                            stat[:, s + 8:s + 9], stat[:, s + 8:s + 9], AF.Identity,
                                            bias=stat[:, s + 11:s + 12], scale=1.0),
                                        inc=se, drain=False)
                        tok_rc = P.emit("vector", [(se, tok_la)],
                               lambda e, s=sa: e.reciprocal(stat[:, s + 9:s + 10], stat[:, s + 8:s + 9]),
                               inc=ve, drain=False)
                    else:
                        P.emit("vector", [(se, tok_ts)],
                               lambda e, s=sa, hf_=hf: e.scalar_tensor_tensor(
                                   stat[:, s + 8:s + 9], stat[:, s + 7:s + 8], kcol[:, hf_:hf_ + 1],
                                   stat[:, s + 11:s + 12], ALU.mult, ALU.add),
                               inc=ve, drain=False)
                        tok_rc = P.emit("vector", [],
                               lambda e, s=sa: e.reciprocal(stat[:, s + 9:s + 10], stat[:, s + 8:s + 9]),
                               inc=ve)
                    tok_rl2 = P.emit("vector", [],
                           lambda e, s=sa, hf_=hf: e.tensor_scalar(
                               stat[:, s + 10:s + 11], stat[:, s + 9:s + 10], kcol[:, hf_:hf_ + 1], None, ALU.mult),
                           inc=ve)
                    P.emit("vector", [(gp, tok_mk)],
                           lambda e, s_=slot, dr_=dri, s=sa: e.tensor_scalar(
                               aoT[:, PH0 + s_ * 2048: PH0 + s_ * 2048 + dr_ * 512],
                               aoT[:, PH0 + s_ * 2048: PH0 + s_ * 2048 + dr_ * 512],
                               stat[:, s + 9:s + 10], None, ALU.mult),
                           inc=ve, drain=False)
                    tok_pn = P.emit("vector", [],
                                    lambda e, s_=slot, dr_=dri, nv_=nv, s=sa: e.tensor_scalar(
                                        aoT[:, PH0 + s_ * 2048 + dr_ * 512: PH0 + s_ * 2048 + nv_ * 512],
                                        aoT[:, PH0 + s_ * 2048 + dr_ * 512: PH0 + s_ * 2048 + nv_ * 512],
                                        stat[:, s + 10:s + 11], None, ALU.mult),
                                    inc=ve)
                    phat_ve[slot] = tok_pn

                class TavState:
                    pass

                def tav_begin(grp):
                    st = TavState()
                    st.grp = grp
                    st.tiles = (2 * grp, 2 * grp + 1)
                    st.hf = grp // 4
                    st.ob = grp % 2
                    dr2 = (st.tiles[1] % 8) // 4 + 1
                    st.klist = [(st.hf * 8 + k, k) for k in range(2 * (grp % 4) + 2)]
                    st.klist += [((1 - st.hf) * 8 + k8, dr2 * 4 + k8) for k8 in range(8)]
                    st.nk = len(st.klist)
                    st.pending = None
                    return st

                def tav_av(st, ki, kseq, ptb, ev):
                    wmm = [ev]
                    if ki == 0:
                        wmm.append((ve, o_ve[st.ob] if o_ve[st.ob] > 0 else csx_tok))
                    def mkav(e, ks=kseq, ob_=st.ob, first=(ki == 0), last=(ki == st.nk - 1), p_=ptb, vb_=vb):
                        e.ldweights(POOL_bf[:, vb_ + ks * 128: vb_ + ks * 128 + 128])
                        return e.matmul(psB[:, ob_ * 512: ob_ * 512 + 256],
                                        POOL_bf[:, vb_ + ks * 128: vb_ + ks * 128 + 128],
                                        POOL_bf[:, PT0 + p_ * 512: PT0 + p_ * 512 + 256],
                                        start=first, stop=last)
                    tok_av = P.emit("tensor", [(sm, v) for (sm, v) in wmm if v > 0], mkav, inc=pe)
                    pt_pe[ptb] = tok_av
                    return tok_av

                def tav_steps(st, k_from, k_to):
                    for ki in range(k_from, k_to):
                        kseq, kph = st.klist[ki]
                        ptb = ki % 2
                        b = tb_next[0]
                        tb_next[0] ^= 1
                        wt = []
                        if tb_ve[b] != 0:
                            wt.append(tb_ve[b])
                        for ti, i in enumerate(st.tiles):
                            slot = i % 4
                            wt2 = list(wt) + [(ve, phat_ve[slot])]
                            tok_t = P.emit("tensor", wt2,
                                           lambda e, s_=slot, kp=kph, b_=b, ti_=ti: e.transpose(
                                               psT[:, b_ * 1024 + ti_ * 128: b_ * 1024 + ti_ * 128 + 128],
                                               aoT[:, PH0 + s_ * 2048 + kp * 128: PH0 + s_ * 2048 + kp * 128 + 128],
                                               ident[:, :]),
                                           inc=pe)
                            phat_pe[slot] = tok_t
                            wt = []
                        tb_pe[b] = tok_t
                        wv = [(pe, tok_t)]
                        if pt_pe[ptb] > 0:
                            wv.append((pe, pt_pe[ptb]))
                        tok_ev = P.emit("vector", wv,
                                        lambda e, b_=b, p_=ptb: e.tensor_copy(
                                            POOL_bf[:, PT0 + p_ * 512: PT0 + p_ * 512 + 256],
                                            psT[:, b_ * 1024: b_ * 1024 + 256]),
                                        inc=ve, drain=False)
                        ev = (ve, tok_ev)
                        tb_ve[b] = ev
                        pt_ve[ptb] = tok_ev
                        if st.pending is not None:
                            tav_av(st, *st.pending)
                        st.pending = (ki, kseq, ptb, ev)

                def tav_end(st):
                    tok_av = tav_av(st, *st.pending)
                    st.pending = None
                    o_pe[st.ob] = tok_av
                    grp = st.grp
                    if st.hf == 0:
                        o_ve[st.ob] = P.emit("vector", [(pe, tok_av)],
                                             lambda e, ha=hloc, g_=grp, ob_=st.ob: e.tensor_copy(
                                                 aoT[:, ha * 1024 + g_ * 256: ha * 1024 + g_ * 256 + 256],
                                                 psB[:, ob_ * 512: ob_ * 512 + 256]),
                                             inc=ve, drain=False)
                    else:
                        sst = hloc % 2
                        wsv = [(pe, tok_av)]
                        if grp == 4 and snd_tok[sst] > 0:
                            wsv.append((s_snd[sst], snd_tok[sst]))
                        o_ve[st.ob] = P.emit("vector", wsv,
                                             lambda e, g_=grp, ob_=st.ob, ss=sst: e.tensor_copy(
                                                 POOL_bf[:, SSTG + ss * 1024 + (g_ - 4) * 256: SSTG + ss * 1024 + (g_ - 4) * 256 + 256],
                                                 psB[:, ob_ * 512: ob_ * 512 + 256]),
                                             inc=ve, drain=False)

                prev = None
                for grp in range(8):
                    e_a = emit_softmax_pre(2 * grp)
                    if prev is not None:
                        tav_steps(prev, 0, prev.nk // 2)
                    e_b = emit_softmax_pre(2 * grp + 1)
                    if prev is not None:
                        tav_steps(prev, prev.nk // 2, prev.nk)
                        tav_end(prev)
                    emit_softmax_post(2 * grp, *e_a)
                    emit_softmax_post(2 * grp + 1, *e_b)
                    prev = tav_begin(grp)
                tav_steps(prev, 0, prev.nk)
                tav_end(prev)

                sst = hloc % 2
                snd_tok[sst] = dma("sync", s_snd[sst], dsend[hloc],
                                   POOL_bf[:, SSTG + sst * 1024: SSTG + sst * 1024 + 1024],
                                   waits=[(ve, o_ve[0]), (ve, o_ve[1])])
            qkv_free_ve[0] = qkv_free_ve[1] = ve.n
            qkv_free_se[0] = qkv_free_se[1] = se.n
            qkv_free_gp[0] = qkv_free_gp[1] = gp.n
        att_done_ve = ve.n
        att_done_se = se.n
        att_done_pe = pe.n

        merge_tok = 0
        if KLIMIT >= 30:
            cc_tok = P.emit("gpsimd", [(s_snd[0], snd_tok[0]), (s_snd[1], snd_tok[1])],
                            lambda e: e.collective_compute(
                                "AllGather", ALU.bypass,
                                replica_groups=[[0, 1], [2, 3], [4, 5], [6, 7]],
                                ins=[dsend], outs=[dgath]),
                            inc=gp)
            r0 = dma("sync", s_rcv, aoT[:, 8192:16384],
                     dgath[0].rearrange("h p f -> p h f"),
                     waits=[(gp, cc_tok), (pe, att_done_pe), (ve, att_done_ve)])
            r1 = dma("sync", s_rcv, POOL_bf[:, 0:8192],
                     dgath[1].rearrange("h p f -> p h f"), waits=[])
            P.emit("vector", [(s_rcv, r1)],
                   lambda e: e.tensor_scalar(aoT[:, 8192:16384], aoT[:, 8192:16384],
                                             kcol[:, 0:1], None, ALU.mult), inc=ve)
            merge_tok = P.emit("vector", [],
                               lambda e: e.scalar_tensor_tensor(
                                   aoT[:, 8192:16384], POOL_bf[:, 0:8192], kcol[:, 1:2],
                                   aoT[:, 8192:16384], ALU.mult, ALU.add), inc=ve)

        WG0, WU0, WD0, WO0 = 0, 4096, 8192, 19456
        gu_tok = [0, 0]
        gu_free = [0, 0]
        wd_tok = [0, 0]
        wd_free = [0, 0]
        wo_tok = [0]
        wo_free = [0]
        xf_tok = [0, 0]
        xf_free = [0, 0]
        gps_ve = [0, 0]
        gps_se = [0, 0]
        hid_read_pe = [0]
        out_tok = [0, 0]
        silu_free = [0]

        for tb in range(2 if KLIMIT >= 40 else 0):
            for ht in range(NC_H):
                ping = ht % 2
                w = [(pe, gu_free[ping])] if gu_tok[ping] > 0 else [(pe, att_done_pe), (se, att_done_se)]
                dma("gpsimd", s_wg[ping], RS_bf[:, WG0 + ping * 2048: WG0 + ping * 2048 + 2048], dwg[ht], waits=w)
                gu_tok[ping] = dma("gpsimd", s_wg[ping], RS_bf[:, WU0 + ping * 2048: WU0 + ping * 2048 + 2048], dwu[ht])
                gcol = ping * 1024
                ucol = ping * 1024 + 512
                w = [(s_wg[ping], gu_tok[ping]), (ve, csx_tok), (ve, xnT_tok), (ve, att_done_ve)]
                if gps_ve[ping] > 0:
                    w += [(ve, gps_ve[ping]), (se, gps_se[ping])]
                for c in range(NC_D):
                    def mkg(e, c_=c, g_=gcol, p_=ping, tb_=tb):
                        e.ldweights(RS_bf[:, WG0 + p_ * 2048 + c_ * 128: WG0 + p_ * 2048 + c_ * 128 + 128])
                        return e.matmul(psA[:, g_:g_ + 512],
                                        RS_bf[:, WG0 + p_ * 2048 + c_ * 128: WG0 + p_ * 2048 + c_ * 128 + 128],
                                        xnT[:, c_ * T + tb_ * 512: c_ * T + tb_ * 512 + 512],
                                        start=(c_ == 0), stop=False)
                    P.emit("tensor", w if c == 0 else [], mkg, inc=None)
                P.emit("tensor", [], lambda e, g_=gcol, tb_=tb: e.matmul(
                    psA[:, g_:g_ + 512], ones_r[0:1, :], csx_hi[0:1, tb_ * 512: tb_ * 512 + 512], start=False, stop=False), inc=None)
                P.emit("tensor", [], lambda e, g_=gcol, tb_=tb: e.matmul(
                    psA[:, g_:g_ + 512], ones_r[0:1, :], csx_lo[0:1, tb_ * 512: tb_ * 512 + 512], start=False, stop=True), inc=None)
                for c in range(NC_D):
                    def mku(e, c_=c, u_=ucol, p_=ping, tb_=tb):
                        e.ldweights(RS_bf[:, WU0 + p_ * 2048 + c_ * 128: WU0 + p_ * 2048 + c_ * 128 + 128])
                        return e.matmul(psA[:, u_:u_ + 512],
                                        RS_bf[:, WU0 + p_ * 2048 + c_ * 128: WU0 + p_ * 2048 + c_ * 128 + 128],
                                        xnT[:, c_ * T + tb_ * 512: c_ * T + tb_ * 512 + 512],
                                        start=(c_ == 0), stop=False)
                    P.emit("tensor", [], mku, inc=None)
                P.emit("tensor", [], lambda e, u_=ucol, tb_=tb: e.matmul(
                    psA[:, u_:u_ + 512], ones_r[0:1, :], csx_hi[0:1, tb_ * 512: tb_ * 512 + 512], start=False, stop=False), inc=None)
                tok_up = P.emit("tensor", [], lambda e, u_=ucol, tb_=tb: e.matmul(
                    psA[:, u_:u_ + 512], ones_r[0:1, :], csx_lo[0:1, tb_ * 512: tb_ * 512 + 512], start=False, stop=True), inc=pe)
                gu_free[ping] = tok_up
                tok_sg = P.emit("scalar", [(pe, tok_up), (ve, silu_free[0])],
                                lambda e, g_=gcol, ht_=ht: e.activation(
                                    RS[:, SILU_G:SILU_G + 512], psA[:, g_:g_ + 512], AF.Silu, scale=sgc[:, ht_:ht_ + 1]),
                                inc=se, drain=False)
                tok_hid = P.emit("vector", [(se, tok_sg), (pe, hid_read_pe[0]), (ve, merge_tok)],
                                 lambda e, u_=ucol, ht_=ht: e.scalar_tensor_tensor(
                                     POOL_bf[:, HID0 + ht_ * 512: HID0 + ht_ * 512 + 512],
                                     psA[:, u_:u_ + 512], suc[:, ht_:ht_ + 1],
                                     RS[:, SILU_G:SILU_G + 512], ALU.mult, ALU.mult),
                                 inc=ve, drain=False)
                gps_ve[ping] = tok_hid
                gps_se[ping] = tok_sg
                silu_free[0] = tok_hid
            hid_all = ve.n

            w0 = [(ve, hid_all)]
            if o_ve[0] > 0:
                w0 += [(ve, o_ve[0]), (ve, o_ve[1])]
            for c in range(NC_H):
                P.emit("tensor", w0 if c == 0 else [],
                       lambda e, c_=c: e.matmul(psB[0:1, 0:512], ones_c[:, :],
                                                POOL_bf[:, HID0 + c_ * 512: HID0 + c_ * 512 + 512],
                                                start=(c_ == 0), stop=(c_ == NC_H - 1)),
                       inc=pe if c == NC_H - 1 else None)
            tok_csh = P.emit("vector", [(pe, pe.n)],
                             lambda e: e.tensor_copy(csh[:, :], psB[0:1, 0:512]), inc=ve)
            tok_n = P.emit("vector", [(ve, tok_csh)],
                           lambda e: e.tensor_scalar(csh[:, :], csh[:, :], -128.0, None, ALU.mult), inc=ve)
            tok_hi2 = P.emit("vector", [(ve, tok_n)],
                             lambda e: e.tensor_copy(csh_hi[:, :], csh[:, :]), inc=ve)
            csh_tok = P.emit("vector", [(ve, tok_hi2)],
                             lambda e: e.tensor_tensor(csh_lo[:, :], csh[:, :], csh_hi[:, :], ALU.subtract), inc=ve)

            for dt in range(16):
                ping = dt % 2
                w = [(pe, wd_free[ping])] if wd_tok[ping] > 0 else []
                wd_tok[ping] = dma("gpsimd", s_wd[ping], RS_bf[:, WD0 + ping * 5632: WD0 + ping * 5632 + 5632], dwd[dt], waits=w)
                w = [(pe, wo_free[0])] if wo_tok[0] > 0 else []
                wo_tok[0] = dma("gpsimd", s_wo, RS_bf[:, WO0:WO0 + 2048], dwout[dt], waits=w)
                if xf_tok[ping] > 0:
                    w = [(ve, xf_free[ping]), (s_out[ping], out_tok[ping])]
                else:
                    w = [(pe, att_done_pe)]
                xf_tok[ping] = dma("sync", s_xf[ping], RS[:, XF0 + ping * 512: XF0 + ping * 512 + 512],
                                   dxT[dt * 128: dt * 128 + 128, tb * 512: tb * 512 + 512], waits=w)

                gp_col = ping * 1024
                sp_col = ping * 1024 + 512
                wmm = [(s_wd[ping], wd_tok[ping]), (ve, csh_tok)]
                if gps_ve[ping] > 0:
                    wmm += [(ve, gps_ve[ping]), (se, gps_se[ping])]
                for c in range(NC_H):
                    def mkd(e, c_=c, g_=gp_col, p_=ping):
                        e.ldweights(RS_bf[:, WD0 + p_ * 5632 + c_ * 128: WD0 + p_ * 5632 + c_ * 128 + 128])
                        return e.matmul(psA[:, g_:g_ + 512],
                                        RS_bf[:, WD0 + p_ * 5632 + c_ * 128: WD0 + p_ * 5632 + c_ * 128 + 128],
                                        POOL_bf[:, HID0 + c_ * 512: HID0 + c_ * 512 + 512],
                                        start=(c_ == 0), stop=False)
                    P.emit("tensor", wmm if c == 0 else [], mkd, inc=None)
                P.emit("tensor", [], lambda e, g_=gp_col: e.matmul(
                    psA[:, g_:g_ + 512], ones_r[0:1, :], csh_hi[0:1, :], start=False, stop=False), inc=None)
                tok_d = P.emit("tensor", [], lambda e, g_=gp_col: e.matmul(
                    psA[:, g_:g_ + 512], ones_r[0:1, :], csh_lo[0:1, :], start=False, stop=True), inc=pe)
                hid_read_pe[0] = tok_d
                wd_free[ping] = tok_d
                for c in range(NC_D):
                    def mko(e, c_=c, s_=sp_col, tb_=tb):
                        e.ldweights(RS_bf[:, WO0 + c_ * 128: WO0 + c_ * 128 + 128])
                        return e.matmul(psA[:, s_:s_ + 512],
                                        RS_bf[:, WO0 + c_ * 128: WO0 + c_ * 128 + 128],
                                        aoT[:, c_ * 1024 + tb_ * 512: c_ * 1024 + tb_ * 512 + 512],
                                        start=(c_ == 0), stop=(c_ == NC_D - 1))
                    P.emit("tensor", [(s_wo, wo_tok[0]), (ve, att_done_ve), (ve, merge_tok)] if c == 0 else [], mko,
                           inc=pe if c == NC_D - 1 else None)
                tok_s = pe.n
                wo_free[0] = tok_s
                tok_g = P.emit("scalar", [(pe, tok_d), (ve, silu_free[0])],
                               lambda e, g_=gp_col, dt_=dt: e.activation(
                                   RS[:, SILU_G:SILU_G + 512],
                                   psA[:, g_:g_ + 512], AF.Silu, scale=sdc[:, dt_:dt_ + 1]),
                               inc=se, drain=False)
                tok_p = P.emit("vector", [(se, tok_g), (pe, tok_s)],
                               lambda e, s_=sp_col: e.tensor_tensor(
                                   RS[:, SILU_G:SILU_G + 512], psA[:, s_:s_ + 512],
                                   RS[:, SILU_G:SILU_G + 512], ALU.mult),
                               inc=ve, drain=False)
                tok_o = P.emit("vector", [(ve, tok_p), (s_xf[ping], xf_tok[ping])],
                               lambda e, p_=ping: e.tensor_tensor(
                                   RS[:, XF0 + p_ * 512: XF0 + p_ * 512 + 512],
                                   RS[:, SILU_G:SILU_G + 512],
                                   RS[:, XF0 + p_ * 512: XF0 + p_ * 512 + 512], ALU.add),
                               inc=ve)
                xf_free[ping] = tok_o
                silu_free[0] = tok_o
                gps_ve[ping] = tok_o
                gps_se[ping] = tok_g
                out_tok[ping] = dma("sync", s_out[ping], dout[dt * 128: dt * 128 + 128, tb * 512: tb * 512 + 512],
                                    RS[:, XF0 + ping * 512: XF0 + ping * 512 + 512], waits=[(ve, tok_o)])

        if DEBUG:
            tdbg = dma("sync", s_cos, d_xnT, xnT[:, :], waits=[(ve, xnT_tok)])
            tdbg = dma("sync", s_cos, d_aoT, aoT[:, :], waits=[(ve, att_done_ve)])
            P.emit("sync", [(s_cos, tdbg)], lambda e: e.nop(), inc=None)
        P.emit("sync", [(s_out[0], out_tok[0]), (s_out[1], out_tok[1])], lambda e: e.nop(), inc=None)

        with nc.Block() as block:
            @block.sync
            def _(e):
                for f in P.ops["sync"]:
                    f(e)

            @block.gpsimd
            def _(e):
                for f in P.ops["gpsimd"]:
                    f(e)

            @block.tensor
            def _(e):
                for f in P.ops["tensor"]:
                    f(e)

            @block.vector
            def _(e):
                for f in P.ops["vector"]:
                    f(e)

            @block.scalar
            def _(e):
                for f in P.ops["scalar"]:
                    f(e)

    return nc


def _host_prep(inputs):
    x = np.asarray(inputs["L0_attn_x"], dtype=np.float32)
    w_qkv = np.asarray(inputs["L0_attn_w_qkv"], dtype=np.float32)
    w_out = np.asarray(inputs["L0_attn_w_out"], dtype=np.float32)
    qg = np.asarray(inputs["L0_attn_q_gamma"], dtype=np.float32)
    kg = np.asarray(inputs["L0_attn_k_gamma"], dtype=np.float32)
    wqg = np.asarray(inputs["L0_ffn_wq_gate"])
    wqu = np.asarray(inputs["L0_ffn_wq_up"])
    wqd = np.asarray(inputs["L0_ffn_wq_down"])
    sg = np.asarray(inputs["L0_ffn_s_gate"], dtype=np.float32)
    su = np.asarray(inputs["L0_ffn_s_up"], dtype=np.float32)
    sd = np.asarray(inputs["L0_ffn_s_down"], dtype=np.float32)

    wq = w_qkv[:, 0:2048].reshape(D, NH, HD)
    wk = w_qkv[:, 2048:4096].reshape(D, NH, HD)
    wv = w_qkv[:, 4096:6144].reshape(D, NH, HD)
    wqkv_all = np.empty((8, D, 768), dtype=np.float32)
    for p in range(8):
        wqkv_all[p, :, 0:128] = wk[:, 2 * p]
        wqkv_all[p, :, 128:256] = wk[:, 2 * p + 1]
        wqkv_all[p, :, 256:384] = wv[:, 2 * p]
        wqkv_all[p, :, 384:512] = wv[:, 2 * p + 1]
        wqkv_all[p, :, 512:640] = wq[:, 2 * p]
        wqkv_all[p, :, 640:768] = wq[:, 2 * p + 1]
    def wqkv_for(g):
        wt = wqkv_all[4 * g:4 * g + 4]
        return np.ascontiguousarray(
            wt.reshape(2, 2, NC_D, 128, 768).transpose(0, 3, 1, 2, 4))
    def wout_for(g):
        blk = w_out.reshape(16, 128, D)
        order = list(range(8 * g, 8 * g + 8)) + list(range(8 * (1 - g), 8 * (1 - g) + 8))
        wp = blk[order].reshape(D, D)
        return np.ascontiguousarray(wp.reshape(16, 128, 16, 128).transpose(2, 1, 0, 3))

    def u8(a):
        return a.astype(np.int32).view(np.uint8)[..., 0::4]

    wgT = np.ascontiguousarray(u8(wqg).T)
    wuT = np.ascontiguousarray(u8(wqu).T)
    wdT = np.ascontiguousarray(u8(wqd).T)
    wg8 = np.ascontiguousarray(wgT.reshape(16, 128, NC_H, 128).transpose(2, 1, 0, 3))
    wu8 = np.ascontiguousarray(wuT.reshape(16, 128, NC_H, 128).transpose(2, 1, 0, 3))
    wd8 = np.ascontiguousarray(wdT.reshape(NC_H, 128, 16, 128).transpose(2, 1, 0, 3))

    sgc = np.ascontiguousarray(sg.reshape(NC_H, 128).T)
    suc = np.ascontiguousarray(su.reshape(NC_H, 128).T)
    sdc = np.ascontiguousarray(sd.reshape(16, 128).T)
    qgb = np.ascontiguousarray(np.broadcast_to(qg, (128, 128)))
    kgb = np.ascontiguousarray(np.broadcast_to(kg, (128, 128)))

    inv_freq = 1.0 / (10000.0 ** (np.arange(0, HD, 2, dtype=np.float64) / HD))
    tpos = np.arange(T, dtype=np.float64)
    freqs = tpos[:, None] * inv_freq[None, :]
    emb = np.concatenate([freqs, freqs], axis=1)
    cos_t = np.cos(emb)
    ssin_t = np.sin(emb)
    ssin_t[:, 0:64] = -ssin_t[:, 0:64]

    diagm = np.zeros((128, 640), dtype=np.float32)
    qi = np.arange(128)[:, None]
    ki = np.arange(128)[None, :]
    diagm[:, 0:128] = np.where(qi >= ki, 1.0, 0.0)
    diagm[:, 128:640] = 0.0
    diagm = diagm.astype(ml_dtypes.bfloat16)

    ident = np.eye(128, dtype=np.float32).astype(ml_dtypes.bfloat16)

    per_core = []
    for core in range(8):
        b, g = core // 2, core % 2
        xb = x[b]
        if g == 0:
            xp = xb
            pos = np.arange(T)
        else:
            xp = np.ascontiguousarray(np.concatenate([xb[1024:], xb[:1024]], axis=0))
            pos = np.concatenate([np.arange(1024, 2048), np.arange(1024)])
        cosp = np.ascontiguousarray(cos_t[pos].astype(np.float16).reshape(16, 128, 128).transpose(1, 0, 2))
        sinp = np.ascontiguousarray(ssin_t[pos].astype(np.float16).reshape(16, 128, 128).transpose(1, 0, 2))
        kc = np.ascontiguousarray(np.broadcast_to(
            np.array([float(g), float(1 - g)], dtype=np.float32), (128, 2)))
        xT_own = np.ascontiguousarray(xp[:1024].T)
        per_core.append(dict(
            x=np.ascontiguousarray(xp), xT=xT_own,
            wqkv=wqkv_for(g), wout=wout_for(g), wg8=wg8, wu8=wu8, wd8=wd8,
            sgc=sgc, suc=suc, sdc=sdc, qgb=qgb, kgb=kgb,
            cosq=cosp, ssin=sinp, diagm=diagm, ident=ident, kcol=kc,
        ))
    return per_core


def kernel(**inputs):
    per_core = _host_prep(inputs)
    nc = build_program()
    res = run_bass_kernel_spmd(nc, per_core, core_ids=list(range(8)))
    out = np.empty((4, 2048, 2048), dtype=np.float32)
    for core in range(8):
        b, g = core // 2, core % 2
        shard = res.results[core]["out"]
        out[b, g * 1024:(g + 1) * 1024, :] = np.asarray(shard).T
    return out


# version 23
# speedup vs baseline: 1.1633x; 1.0300x over previous
import os
import numpy as np
import ml_dtypes
import contextlib
import concourse.bass as bass
import concourse.mybir as mybir
from concourse.bass_utils import run_bass_kernel_spmd

F32 = mybir.dt.float32
F16 = mybir.dt.float16
BF16 = mybir.dt.bfloat16
U8 = mybir.dt.uint8
AF = mybir.ActivationFunctionType
ALU = mybir.AluOpType
AX = mybir.AxisListType

T = 2048
D = 2048
NH = 16
HD = 128
NC_D = 16
NC_H = 44
OWN = 1024
NEG = -1e30
C_ATT = float(1.0 / np.sqrt(128.0))
NPAIR = 4
KLIMIT = int(os.environ.get("KLIMIT", "99"))
KPAIRS = int(os.environ.get("KPAIRS", str(NPAIR)))
KSUB = int(os.environ.get("KSUB", "9"))


class Sem:
    def __init__(self, h):
        self.h = h
        self.n = 0


class Plan:
    ENGS = ("sync", "gpsimd", "tensor", "vector", "scalar")

    def __init__(self):
        self.ops = {e: [] for e in self.ENGS}

    DRAIN = frozenset(("vector", "scalar", "gpsimd"))

    def emit(self, eng, waits, fn, inc=None, inc_by=1, drain=None):
        tok = None
        if inc is not None:
            inc.n += inc_by
            tok = inc.n
        if drain is None:
            drain = eng in self.DRAIN
        def op(e, waits=tuple(waits), fn=fn, inc=inc, inc_by=inc_by, drain=drain):
            for s, v in waits:
                if v > 0:
                    e.wait_ge(s.h, v)
            if drain:
                e.drain()
            ins = fn(e)
            if inc is not None:
                ins.then_inc(inc.h, inc_by)
        self.ops[eng].append(op)
        return tok


def build_program():
    nc = bass.Bass("TRN2", target_bir_lowering=False)

    dx = nc.dram_tensor("x", [T, D], F32, kind="ExternalInput").ap()
    dxT = nc.dram_tensor("xT", [D, OWN], F32, kind="ExternalInput").ap()
    dwqkv = nc.dram_tensor("wqkv", [NPAIR // 2, 128, 2, NC_D, 768], F32, kind="ExternalInput").ap()
    dwout = nc.dram_tensor("wout", [16, 128, NC_D, 128], F32, kind="ExternalInput").ap()
    dwg = nc.dram_tensor("wg8", [NC_H, 128, NC_D, 128], U8, kind="ExternalInput").ap()
    dwu = nc.dram_tensor("wu8", [NC_H, 128, NC_D, 128], U8, kind="ExternalInput").ap()
    dwd = nc.dram_tensor("wd8", [16, 128, NC_H, 128], U8, kind="ExternalInput").ap()
    dsg = nc.dram_tensor("sgc", [128, NC_H], F32, kind="ExternalInput").ap()
    dsu = nc.dram_tensor("suc", [128, NC_H], F32, kind="ExternalInput").ap()
    dsd = nc.dram_tensor("sdc", [128, 16], F32, kind="ExternalInput").ap()
    dqg = nc.dram_tensor("qgb", [128, 128], F32, kind="ExternalInput").ap()
    dkg = nc.dram_tensor("kgb", [128, 128], F32, kind="ExternalInput").ap()
    dcos = nc.dram_tensor("cosq", [128, 16, 128], F16, kind="ExternalInput").ap()
    dsin = nc.dram_tensor("ssin", [128, 16, 128], F16, kind="ExternalInput").ap()
    dkcol = nc.dram_tensor("kcol", [128, 2], F32, kind="ExternalInput").ap()
    ddiag = nc.dram_tensor("diagm", [128, 640], BF16, kind="ExternalInput").ap()
    dident = nc.dram_tensor("ident", [128, 128], BF16, kind="ExternalInput").ap()
    dout = nc.dram_tensor("out", [D, OWN], F32, kind="ExternalOutput").ap()
    dsend = nc.dram_tensor("sendb", [8, 128, 1024], BF16).ap()
    dgath = nc.dram_tensor("gathb", [2, 8, 128, 1024], BF16).ap()
    DEBUG = False
    if DEBUG:
        d_xnT = nc.dram_tensor("dbg_xnT", [128, NC_D * T], BF16, kind="ExternalOutput").ap()
        d_aoT = nc.dram_tensor("dbg_aoT", [128, NH * OWN], BF16, kind="ExternalOutput").ap()
    else:
        d_xnT = d_aoT = None

    P = Plan()
    with contextlib.ExitStack() as st:
        E = st.enter_context

        xnT = E(nc.sbuf_tensor("xnT", [128, NC_D * T], BF16))
        RS = E(nc.sbuf_tensor("RS", [128, 12416], F32))
        aoT = E(nc.sbuf_tensor("aoT", [128, NH * OWN], BF16))
        POOL = E(nc.sbuf_tensor("POOL", [128, 11264], F32))

        qgb = E(nc.sbuf_tensor("s_qgb", [128, 128], BF16))
        kgb = E(nc.sbuf_tensor("s_kgb", [128, 128], BF16))
        cosq = E(nc.sbuf_tensor("s_cosq", [128, 16 * 128], F16))
        ssin = E(nc.sbuf_tensor("s_ssin", [128, 16 * 128], F16))
        diagm = E(nc.sbuf_tensor("s_diagm", [128, 640], BF16))
        ident = E(nc.sbuf_tensor("s_ident", [128, 128], BF16))
        kcol = E(nc.sbuf_tensor("s_kcol", [128, 2], F32))
        ones_c = E(nc.sbuf_tensor("ones_c", [128, 1], BF16))
        ones_r = E(nc.sbuf_tensor("ones_r", [1, 128], BF16))
        sgc = E(nc.sbuf_tensor("s_sgc", [128, NC_H], F32))
        suc = E(nc.sbuf_tensor("s_suc", [128, NC_H], F32))
        sdc = E(nc.sbuf_tensor("s_sdc", [128, 16], F32))
        stat = E(nc.sbuf_tensor("stat", [128, 64], F32))
        csx_hi = E(nc.sbuf_tensor("csx_hi", [1, 1024], BF16))
        csx_lo = E(nc.sbuf_tensor("csx_lo", [1, 1024], BF16))
        csh_hi = E(nc.sbuf_tensor("csh_hi", [1, 512], BF16))
        csh_lo = E(nc.sbuf_tensor("csh_lo", [1, 512], BF16))

        psA = E(nc.psum_tensor("psA", [128, 2048], F32))
        psB = E(nc.psum_tensor("psB", [128, 1024], F32))
        psT = E(nc.psum_tensor("psT", [128, 2048], BF16))

        pe = Sem(E(nc.semaphore("pe")))
        ve = Sem(E(nc.semaphore("ve")))
        se = Sem(E(nc.semaphore("se")))
        gp = Sem(E(nc.semaphore("gp")))
        s_cst = Sem(E(nc.semaphore("s_cst")))
        s_cos = Sem(E(nc.semaphore("s_cos")))
        s_x = [Sem(E(nc.semaphore("s_x0"))), Sem(E(nc.semaphore("s_x1")))]
        s_slab = Sem(E(nc.semaphore("s_slab")))
        s_wg = [Sem(E(nc.semaphore("s_wg0"))), Sem(E(nc.semaphore("s_wg1")))]
        s_wd = [Sem(E(nc.semaphore("s_wd0"))), Sem(E(nc.semaphore("s_wd1")))]
        s_wo = Sem(E(nc.semaphore("s_wo")))
        s_xf = [Sem(E(nc.semaphore("s_xf0"))), Sem(E(nc.semaphore("s_xf1")))]
        s_out = [Sem(E(nc.semaphore("s_out0"))), Sem(E(nc.semaphore("s_out1")))]
        s_snd = [Sem(E(nc.semaphore("s_snd0"))), Sem(E(nc.semaphore("s_snd1")))]
        s_rcv = Sem(E(nc.semaphore("s_rcv")))

        RS_bf = RS.ap().bitcast(BF16)
        POOL_bf = POOL.ap().bitcast(BF16)

        KP, VP, QP = 0, 4096, 8192
        SSTG = 12288
        PT0 = 14336
        SCRP = [7680, 8448]
        KAP = [9216, 9728]
        QAP = [10240, 10752]
        HID0 = 0
        PH0 = 8192

        def bc2(tbl, t, lo, n):
            return bass.AP(tbl.ap().tensor, t * 128 + lo, [[2048, 128], [0, 2], [1, n]])
        XF0 = 10880
        SILU_G = 11904
        csx = RS[0:1, XF0:XF0 + 1024]
        csh = RS[0:1, SILU_G:SILU_G + 512]

        def dma(eng, sem, out_ap, in_ap, waits=()):
            w = list(waits)
            if sem.n > 0:
                w.append((sem, sem.n))
            return P.emit(eng, w,
                          lambda e, o=out_ap, i=in_ap: e.dma_start(out=o, in_=i),
                          inc=sem, inc_by=16, drain=False)

        tb_pe = [0, 0]
        tb_ve = [0, 0]
        tb_next = [0]

        def transpose_to(src_bf_ap_fn, dst_fn, waits, width=128, evac="vector"):
            b = tb_next[0]
            tb_next[0] ^= 1
            wt = list(waits)
            es = ve if evac == "vector" else se
            if tb_ve[b] != 0:
                wt.append(tb_ve[b])
            tok_t = P.emit("tensor", wt,
                           lambda e, b_=b: e.transpose(psT[:, b_ * 1024: b_ * 1024 + width],
                                                       src_bf_ap_fn(), ident[:, :]),
                           inc=pe)
            tb_pe[b] = tok_t
            if evac == "vector":
                tok_v = P.emit("vector", [(pe, tok_t)],
                               lambda e, b_=b: e.tensor_copy(dst_fn(), psT[:, b_ * 1024: b_ * 1024 + width]),
                               inc=ve, drain=False)
            else:
                tok_v = P.emit("scalar", [(pe, tok_t)],
                               lambda e, b_=b: e.copy(dst_fn(), psT[:, b_ * 1024: b_ * 1024 + width]),
                               inc=se, drain=False)
            tb_ve[b] = (es, tok_v)
            return tok_t, tok_v

        t_const = 0
        for o, i in ((qgb[:, :], dqg), (kgb[:, :], dkg), (diagm[:, :], ddiag), (kcol[:, :], dkcol),
                     (ident[:, :], dident),
                     (sgc[:, :], dsg), (suc[:, :], dsu), (sdc[:, :], dsd)):
            t_const = dma("gpsimd", s_cst, o, i)
        t_cos = dma("sync", s_cos, cosq[:, :], dcos)
        t_cos = dma("sync", s_cos, ssin[:, :], dsin)
        P.emit("gpsimd", [], lambda e: e.memset(ones_c[:, :], 1.0), inc=gp)
        t_ones = P.emit("gpsimd", [], lambda e: e.memset(ones_r[:, :], 1.0), inc=gp)
        for eng in ("tensor", "vector", "scalar"):
            P.emit(eng, [(s_cst, t_const), (gp, t_ones), (s_cos, t_cos)], lambda e: e.nop())

        ln_apply_tok = [0, 0]
        ln_sm_tok = [0, 0]
        ln_sqr_read = [0, 0]
        slab_pe = [0, 0]

        for t in range(16):
            pg = t % 2
            xc = pg * 2048
            sqc = 4096 + pg * 2048
            sb = pg * 10
            w = []
            if t >= 2:
                w = [(se, ln_apply_tok[pg]), (ve, ln_sm_tok[pg])]
            tok_x = dma("sync", s_x[pg], RS[:, xc:xc + 2048],
                        dx.rearrange("(n p) f -> p n f", p=128)[:, t, :], waits=w)
            tok_sq = P.emit("scalar", [(s_x[pg], tok_x)],
                            lambda e, a=sqc, bcol=xc, s=sb: e.activation(
                                RS[:, a:a + 2048], RS[:, bcol:bcol + 2048], AF.Square,
                                accum_out=stat[:, s + 1:s + 2]),
                            inc=se, drain=False)
            ln_sm_tok[pg] = P.emit("vector", [(s_x[pg], tok_x)],
                                   lambda e, bcol=xc, s=sb: e.reduce_sum(stat[:, s:s + 1], RS[:, bcol:bcol + 2048], axis=AX.X),
                                   inc=ve)
            P.emit("vector", [(se, tok_sq)], lambda e, s=sb: e.tensor_scalar(stat[:, s + 2:s + 3], stat[:, s:s + 1], 1.0 / 2048.0, None, ALU.mult), inc=ve)
            P.emit("vector", [], lambda e, s=sb: e.tensor_scalar(stat[:, s + 3:s + 4], stat[:, s + 1:s + 2], 1.0 / 2048.0, None, ALU.mult), inc=ve)
            P.emit("vector", [], lambda e, s=sb: e.tensor_tensor(stat[:, s + 4:s + 5], stat[:, s + 2:s + 3], stat[:, s + 2:s + 3], ALU.mult), inc=ve)
            P.emit("vector", [], lambda e, s=sb: e.tensor_tensor(stat[:, s + 5:s + 6], stat[:, s + 3:s + 4], stat[:, s + 4:s + 5], ALU.subtract), inc=ve)
            tok_v = P.emit("vector", [], lambda e, s=sb: e.tensor_scalar(stat[:, s + 6:s + 7], stat[:, s + 5:s + 6], 1e-5, None, ALU.add), inc=ve)
            wv = [(ve, tok_v)]
            if t >= 2:
                wv.append((ve, ln_sqr_read[pg]))
            tok_sqr = P.emit("scalar", wv, lambda e, s=sb: e.activation(stat[:, s + 7:s + 8], stat[:, s + 6:s + 7], AF.Sqrt), inc=se)
            tok_rs = P.emit("vector", [(se, tok_sqr)], lambda e, s=sb: e.reciprocal(stat[:, s + 8:s + 9], stat[:, s + 7:s + 8]), inc=ve)
            ln_sqr_read[pg] = tok_rs
            tok_bias = P.emit("vector", [],
                              lambda e, s=sb: e.tensor_scalar(stat[:, s + 9:s + 10], stat[:, s + 2:s + 3], stat[:, s + 8:s + 9], -1.0, ALU.mult, ALU.mult),
                              inc=ve)
            xcol = pg * 2048
            w = [(ve, tok_bias)]
            if t >= 2:
                w.append((pe, slab_pe[pg]))
            ln_apply_tok[pg] = P.emit(
                "scalar", w,
                lambda e, bcol=xc, xo=xcol, s=sb: e.activation(POOL_bf[:, xo:xo + 2048], RS[:, bcol:bcol + 2048], AF.Identity,
                                                               bias=stat[:, s + 9:s + 10], scale=stat[:, s + 8:s + 9]),
                inc=se)
            for half in (0, 1):
                for c8 in range(8):
                    c = half * 8 + c8
                    wtr = [(se, ln_apply_tok[pg])] if c8 == 0 else []
                    if c8 == 0 and tb_ve[half] != 0:
                        wtr.append(tb_ve[half])
                    tok_t = P.emit("tensor", wtr,
                                   lambda e, h_=half, c8_=c8, c_=c, xo=xcol: e.transpose(
                                       psT[:, h_ * 1024 + c8_ * 128: h_ * 1024 + c8_ * 128 + 128],
                                       POOL_bf[:, xo + c_ * 128: xo + c_ * 128 + 128],
                                       ident[:, :]),
                                   inc=pe)
                tok_e = P.emit("vector", [(pe, tok_t)],
                               lambda e, h_=half, t_=t: e.tensor_copy(
                                   xnT[:, :].rearrange("p (c f) -> p c f", c=16)[:, h_ * 8:(h_ + 1) * 8, t_ * 128:t_ * 128 + 128],
                                   psT[:, h_ * 1024:(h_ + 1) * 1024].rearrange("p (c f) -> p c f", c=8)),
                               inc=ve, drain=False)
                tb_ve[half] = (ve, tok_e)
            slab_pe[pg] = tok_t
        xnT_tok = max((x[1] if x != 0 else 0) for x in tb_ve)

        for tb in range(2 if KLIMIT >= 2 else 0):
            for c in range(NC_D):
                P.emit("tensor", [(ve, xnT_tok), (gp, t_ones)] if c == 0 else [],
                       lambda e, c_=c, tb_=tb: e.matmul(
                           psB[0:1, tb_ * 512: tb_ * 512 + 512], ones_c[:, :],
                           xnT[:, c_ * T + tb_ * 512: c_ * T + tb_ * 512 + 512],
                           start=(c_ == 0), stop=(c_ == NC_D - 1)),
                       inc=pe if c == NC_D - 1 else None)
            P.emit("vector", [(pe, pe.n)],
                   lambda e, tb_=tb: e.tensor_copy(csx[:, tb_ * 512: tb_ * 512 + 512], psB[0:1, tb_ * 512: tb_ * 512 + 512]),
                   inc=ve)
        tok_n = P.emit("vector", [], lambda e: e.tensor_scalar(csx[:, :], csx[:, :], -128.0, None, ALU.mult), inc=ve)
        tok_hi = P.emit("vector", [(ve, tok_n)], lambda e: e.tensor_copy(csx_hi[:, :], csx[:, :]), inc=ve)
        csx_tok = P.emit("vector", [(ve, tok_hi)], lambda e: e.tensor_tensor(csx_lo[:, :], csx[:, :], csx_hi[:, :], ALU.subtract), inc=ve)
        csum_pe = pe.n

        WSLAB = 24576
        wslab_tok = [0]
        wslab_free_pe = [0]

        qkv_free_ve = [0, 0]
        qkv_free_se = [0, 0]
        qkv_free_gp = [0, 0]

        rmsK_gp = [0, 0]
        rmsK_ve = [0, 0]
        rmsQ_ve = [0, 0]
        khat_gp = [0, 0]
        att_free_se = 0
        att_free_ve = 0
        att_free_sa = [0, 0]
        att_free_sb = [0]
        phat_pe = [0, 0, 0, 0]
        phat_ve = [0, 0, 0, 0]
        o_pe = [0, 0]
        o_ve = [0, 0]
        pt_pe = [0, 0]
        pt_ve = [0, 0]
        rms_it = [0]
        scr_ve = [0, 0]
        scr_gp = [0, 0]
        kh_pe = [0]
        qh_pe = [0]
        snd_tok = [0, 0]

        att_done_ve = 0

        def qkv_evac(t, pcol, tok_mm):
            SCR = SCRP[t % 2]
            KA = KAP[t % 2]
            QA = QAP[t % 2]
            KH = SCR * 2
            QH = SCR * 2 + 1024
            tok_kh = 0
            tok_qh = 0
            wcp = [(pe, tok_mm)]
            if scr_ve[t % 2] > 0:
                wcp += [(ve, scr_ve[t % 2]), (gp, scr_gp[t % 2])]
            tok_cp = P.emit("scalar", wcp,
                            lambda e, pc=pcol, S=SCR: e.copy(POOL[:, S:S + 768], psA[:, pc:pc + 768]),
                            inc=se, drain=False)
            if KSUB >= 3:
                sqk = 16 + (t % 2) * 16
                P.emit("vector", [(se, tok_cp)],
                       lambda e, t_=t: e.tensor_copy(
                           POOL_bf[:, VP:VP + 4096].rearrange("p (h f) -> p h f", h=2)[:, :, t_ * 128:t_ * 128 + 128],
                           POOL[:, SCR + 256:SCR + 512].rearrange("p (h f) -> p h f", h=2)),
                       inc=ve, drain=False)
                if KSUB >= 4:
                    nsl = 4
                    wk = [(gp, scr_gp[0])] if scr_gp[0] > 0 else []
                    P.emit("vector", wk,
                           lambda e: e.tensor_tensor(POOL[:, KA:KA + 256], POOL[:, SCR:SCR + 256],
                                                     POOL[:, SCR:SCR + 256], ALU.mult), inc=ve, drain=False)
                    P.emit("vector", [],
                           lambda e, s=sqk: e.reduce_sum(stat[:, s:s + 2],
                                                         POOL[:, KA:KA + 256].rearrange("p (h f) -> p h f", h=2),
                                                         axis=AX.X), inc=ve)
                    if True:
                        P.emit("vector", [],
                               lambda e: e.tensor_tensor(POOL[:, QA:QA + 256], POOL[:, SCR + 512:SCR + 768],
                                                         POOL[:, SCR + 512:SCR + 768], ALU.mult), inc=ve, drain=False)
                        P.emit("vector", [],
                               lambda e, s=sqk: e.reduce_sum(stat[:, s + 2:s + 4],
                                                             POOL[:, QA:QA + 256].rearrange("p (h f) -> p h f", h=2),
                                                             axis=AX.X), inc=ve)
                    P.emit("vector", [],
                           lambda e, s=sqk: e.tensor_scalar(
                               stat[:, s + 4:s + 6], stat[:, s:s + 2], 1.0 / 128.0, 1e-6,
                               ALU.mult, ALU.add), inc=ve)
                    tok_st = P.emit("vector", [],
                                    lambda e, s=sqk: e.tensor_scalar(
                                        stat[:, s + 6:s + 8], stat[:, s + 2:s + 4], 1.0, 128e-6,
                                        ALU.mult, ALU.add), inc=ve, drain=False)
                    tok_sqr = P.emit("scalar", [(ve, tok_st)],
                                     lambda e, s=sqk, n=nsl: e.activation(stat[:, s + 8:s + 8 + n],
                                                                          stat[:, s + 4:s + 4 + n], AF.Sqrt),
                                     inc=se, drain=False)
                    tok_rc = P.emit("vector", [(se, tok_sqr)],
                                    lambda e, s=sqk, n=nsl: e.reciprocal(stat[:, s + 12:s + 12 + n],
                                                                         stat[:, s + 8:s + 8 + n]), inc=ve)
                    for h in (0, 1):
                        P.emit("vector", [],
                               lambda e, h_=h, s=sqk: e.scalar_tensor_tensor(
                                   POOL[:, SCR + h_ * 128: SCR + h_ * 128 + 128],
                                   POOL[:, SCR + h_ * 128: SCR + h_ * 128 + 128],
                                   stat[:, s + 12 + h_:s + 13 + h_], kgb[:, :], ALU.mult, ALU.mult),
                               inc=ve, drain=(h == 0))
                        if True:
                            P.emit("vector", [],
                                   lambda e, h_=h, s=sqk: e.scalar_tensor_tensor(
                                       POOL[:, SCR + 512 + h_ * 128: SCR + 512 + h_ * 128 + 128],
                                       POOL[:, SCR + 512 + h_ * 128: SCR + 512 + h_ * 128 + 128],
                                       stat[:, s + 14 + h_:s + 15 + h_], qgb[:, :], ALU.mult, ALU.mult),
                                   inc=ve, drain=False)
                    tok_g1 = ve.n
                    if KSUB >= 5:
                        for h in (0, 1):
                            pass
                        wk = [(ve, tok_g1)]
                        if kh_pe[0] > 0:
                            wk.append((pe, kh_pe[0]))
                        P.emit("gpsimd", wk,
                               lambda e, t_=t: e.tensor_tensor(
                                   POOL[:, KA:KA + 256].rearrange("p (h f) -> p h f", h=2),
                                   POOL[:, SCR:SCR + 256].rearrange("p (h f) -> p h f", h=2),
                                   bc2(cosq, t_, 0, 128), ALU.mult), inc=gp, drain=False)
                        P.emit("gpsimd", [],
                               lambda e, t_=t: e.tensor_tensor(
                                   POOL[:, KA + 256:KA + 512].rearrange("p (h f) -> p h f", h=2)[:, :, 0:64],
                                   POOL[:, SCR:SCR + 256].rearrange("p (h f) -> p h f", h=2)[:, :, 64:128],
                                   bc2(ssin, t_, 0, 64),
                                   ALU.mult), inc=gp, drain=False)
                        P.emit("gpsimd", [],
                               lambda e, t_=t: e.tensor_tensor(
                                   POOL[:, KA + 256:KA + 512].rearrange("p (h f) -> p h f", h=2)[:, :, 64:128],
                                   POOL[:, SCR:SCR + 256].rearrange("p (h f) -> p h f", h=2)[:, :, 0:64],
                                   bc2(ssin, t_, 64, 64),
                                   ALU.mult), inc=gp, drain=False)
                        tok_kh = P.emit("gpsimd", [(pe, kh_pe[0])] if kh_pe[0] > 0 else [],
                                        lambda e: e.tensor_tensor(
                                            POOL_bf[:, KH:KH + 256],
                                            POOL[:, KA:KA + 256], POOL[:, KA + 256:KA + 512], ALU.add),
                                        inc=gp)
                        khat_gp[0] = tok_kh
                        if KSUB >= 7:
                            P.emit("vector", [],
                                   lambda e, t_=t: e.tensor_tensor(
                                       POOL[:, QA:QA + 256].rearrange("p (h f) -> p h f", h=2),
                                       POOL[:, SCR + 512:SCR + 768].rearrange("p (h f) -> p h f", h=2),
                                       bc2(cosq, t_, 0, 128), ALU.mult), inc=ve)
                            P.emit("vector", [],
                                   lambda e, t_=t: e.tensor_tensor(
                                       POOL[:, QA + 256:QA + 512].rearrange("p (h f) -> p h f", h=2)[:, :, 0:64],
                                       POOL[:, SCR + 512:SCR + 768].rearrange("p (h f) -> p h f", h=2)[:, :, 64:128],
                                       bc2(ssin, t_, 0, 64),
                                       ALU.mult), inc=ve, drain=False)
                            P.emit("vector", [],
                                   lambda e, t_=t: e.tensor_tensor(
                                       POOL[:, QA + 256:QA + 512].rearrange("p (h f) -> p h f", h=2)[:, :, 64:128],
                                       POOL[:, SCR + 512:SCR + 768].rearrange("p (h f) -> p h f", h=2)[:, :, 0:64],
                                       bc2(ssin, t_, 64, 64),
                                       ALU.mult), inc=ve, drain=False)
                            tok_qh = P.emit("vector", [(pe, qh_pe[0])] if qh_pe[0] > 0 else [],
                                            lambda e: e.tensor_tensor(
                                                POOL_bf[:, QH:QH + 256],
                                                POOL[:, QA:QA + 256], POOL[:, QA + 256:QA + 512], ALU.add),
                                            inc=ve)
                        scr_ve[t % 2] = ve.n
            scr_gp[t % 2] = gp.n

            def flush(t_=t, KH_=KH, QH_=QH, tok_kh_=tok_kh if KSUB >= 5 else 0,
                      tok_qh_=tok_qh if KSUB >= 7 else 0):
                if KSUB >= 6:
                    for h in (0, 1):
                        tk_t, _ = transpose_to(
                            (lambda h_=h: POOL_bf[:, KH_ + h_ * 128:KH_ + h_ * 128 + 128]),
                            (lambda h_=h: POOL_bf[:, KP + h_ * 2048 + t_ * 128: KP + h_ * 2048 + t_ * 128 + 128]),
                            [(gp, tok_kh_)], evac="scalar")
                        kh_pe[0] = tk_t
                if KSUB >= 7:
                    for h in (0, 1):
                        tq_t, _ = transpose_to(
                            (lambda h_=h: POOL_bf[:, QH_ + h_ * 128:QH_ + h_ * 128 + 128]),
                            (lambda h_=h: POOL_bf[:, QP + h_ * 2048 + t_ * 128: QP + h_ * 2048 + t_ * 128 + 128]),
                            [(ve, tok_qh_)], evac="scalar")
                        qh_pe[0] = tq_t
                scr_ve[t_ % 2] = ve.n
            return tok_cp, flush

        for pair in range(KPAIRS if KLIMIT >= 25 else 0):
            pgrp, pl = pair // 2, pair % 2
            if pl == 0:
                w = [(pe, wslab_free_pe[0])] if pair > 0 else [(se, ln_apply_tok[0]), (se, ln_apply_tok[1]), (ve, ln_sm_tok[0]), (ve, ln_sm_tok[1]), (ve, csx_tok)]
                wslab_tok[0] = dma("gpsimd", s_slab, RS_bf[:, 0:WSLAB],
                                   dwqkv[pgrp], waits=w)
            wb = pl * 12288

            pendT = []
            for t in range(16 if KSUB >= 2 else 0):
                ping = t % 2
                pcol = ping * 1024
                w = [(s_slab, wslab_tok[0]), (ve, xnT_tok)]
                if qkv_free_se[ping] > 0:
                    w.append((se, qkv_free_se[ping]))
                for c in range(NC_D):
                    def mk1(e, c_=c, pc=pcol, wb_=wb, t_=t):
                        e.ldweights(xnT[:, c_ * T + t_ * 128: c_ * T + t_ * 128 + 128])
                        return e.matmul(psA[:, pc:pc + 512],
                                        xnT[:, c_ * T + t_ * 128: c_ * T + t_ * 128 + 128],
                                        RS_bf[:, wb_ + c_ * 768: wb_ + c_ * 768 + 512],
                                        start=(c_ == 0), stop=(c_ == NC_D - 1))
                    P.emit("tensor", w if c == 0 else [], mk1, inc=None)
                    if True:
                        def mk2b(e, c_=c, pc=pcol, wb_=wb, t_=t):
                            return e.matmul(psA[:, pc + 512:pc + 768],
                                            xnT[:, c_ * T + t_ * 128: c_ * T + t_ * 128 + 128],
                                            RS_bf[:, wb_ + c_ * 768 + 512: wb_ + c_ * 768 + 768],
                                            start=(c_ == 0), stop=(c_ == NC_D - 1))
                        P.emit("tensor", [], mk2b,
                               inc=pe if c == NC_D - 1 else None)
                tok_mm = pe.n
                if pl == 1 and t == 15:
                    wslab_free_pe[0] = tok_mm

                if pendT:
                    pendT.pop(0)()
                tok_cp, tok_fl = qkv_evac(t, pcol, tok_mm)
                pendT.append(tok_fl)
                qkv_free_se[ping] = tok_cp
            while pendT:
                pendT.pop(0)()
            qkv_done_ve = ve.n
            qkv_done_se = se.n

            for h in ((0, 1) if KLIMIT >= 30 else ()):
                qb = QP + h * 2048
                kb = KP + h * 2048
                vb = VP + h * 2048
                hloc = pair * 2 + h

                def emit_softmax_pre(i):
                    nonlocal att_free_ve
                    slot = i % 4
                    hf = i // 8
                    dri = (i % 8) // 4 + 1
                    nv = dri + 2
                    sub = i % 4
                    w1 = (dri - 1) * 512 + sub * 128
                    w2 = dri * 512 - w1
                    sa = 40 + (i % 2) * 12
                    w = [(ve, qkv_done_ve), (se, qkv_done_se)]
                    if att_free_ve > 0:
                        w.append((ve, att_free_ve))
                    for v in range(nv):
                        if v < dri:
                            kcolv = (hf * 2 + v) * 512
                        else:
                            kcolv = ((1 - hf) * 2 + (v - dri)) * 512
                        def mks(e, i_=i, v_=v, kc=kcolv, qb_=qb, kb_=kb):
                            if v_ == 0:
                                e.ldweights(POOL_bf[:, qb_ + i_ * 128: qb_ + i_ * 128 + 128])
                            return e.matmul(psA[:, v_ * 512: v_ * 512 + 512],
                                            POOL_bf[:, qb_ + i_ * 128: qb_ + i_ * 128 + 128],
                                            POOL_bf[:, kb_ + kc: kb_ + kc + 512],
                                            start=True, stop=True)
                        wv_ = list(w) if v == 0 else []
                        if att_free_sa[0] > 0 and v <= att_free_sa[1]:
                            wv_.append((se, att_free_sa[0] if v < att_free_sa[1] else att_free_sb[0]))
                        elif att_free_sb[0] > 0 and v > att_free_sa[1]:
                            wv_.append((se, att_free_sb[0]))
                        tok_mmv = P.emit("tensor", wv_, mks,
                                         inc=pe if v == nv - 1 else None)
                    tok_mx = P.emit("vector", [(pe, tok_mmv)],
                                    lambda e, s=sa, n_=nv: e.reduce_max(
                                        stat[:, s + 4:s + 5], psA[:, 0:n_ * 512],
                                        axis=AX.X, negate=True),
                                    inc=ve, drain=False)
                    att_free_ve = tok_mx
                    wslot = [(ve, tok_mx)]
                    if phat_pe[slot] > 0:
                        wslot.append((pe, phat_pe[slot]))
                    tok_e1a = 0
                    if w1 > 0:
                        tok_e1a = P.emit("scalar", wslot,
                               lambda e, s_=slot, w1_=w1, s=sa: e.activation(
                                   aoT[:, PH0 + s_ * 2048: PH0 + s_ * 2048 + w1_],
                                   psA[:, 0:w1_], AF.Exp, bias=stat[:, s + 4:s + 5], scale=1.0,
                                   accum_out=stat[:, s + 6:s + 7]),
                               inc=se, drain=False)
                    tok_e1t = P.emit("scalar", wslot if w1 == 0 else [],
                                     lambda e, s_=slot, w1_=w1, dr_=dri, s=sa: e.activation(
                                         aoT[:, PH0 + s_ * 2048 + w1_: PH0 + s_ * 2048 + dr_ * 512],
                                         psA[:, w1_:dr_ * 512], AF.Exp, bias=stat[:, s + 4:s + 5], scale=1.0),
                                     inc=se, drain=False)
                    tok_e1 = P.emit("scalar", [],
                                    lambda e, s_=slot, dr_=dri, nv_=nv, s=sa: e.activation(
                                        aoT[:, PH0 + s_ * 2048 + dr_ * 512: PH0 + s_ * 2048 + nv_ * 512],
                                        psA[:, dr_ * 512:nv_ * 512], AF.Exp, bias=stat[:, s + 4:s + 5], scale=1.0,
                                        accum_out=stat[:, s + 7:s + 8]),
                                    inc=se, drain=False)
                    tok_mk = P.emit("gpsimd", [(se, tok_e1t)],
                           lambda e, s_=slot, w1_=w1, w2_=w2, dr_=dri: e.tensor_tensor(
                               aoT[:, PH0 + s_ * 2048 + w1_: PH0 + s_ * 2048 + dr_ * 512],
                               aoT[:, PH0 + s_ * 2048 + w1_: PH0 + s_ * 2048 + dr_ * 512],
                               diagm[:, 0:w2_], ALU.mult),
                           inc=gp, drain=False)
                    tok_ts = P.emit("scalar", [(gp, tok_mk)],
                                    lambda e, s_=slot, w1_=w1, dr_=dri, s=sa: e.activation(
                                        aoT[:, PH0 + s_ * 2048 + w1_: PH0 + s_ * 2048 + dr_ * 512],
                                        aoT[:, PH0 + s_ * 2048 + w1_: PH0 + s_ * 2048 + dr_ * 512],
                                        AF.Identity, scale=1.0,
                                        accum_out=stat[:, s + 11:s + 12]),
                                    inc=se, drain=False)
                    att_free_sa[0] = tok_e1t
                    att_free_sa[1] = dri
                    att_free_sb[0] = tok_e1
                    return (tok_ts, w1, tok_mk)

                def emit_softmax_post(i, tok_ts, w1, tok_mk):
                    slot = i % 4
                    hf = i // 8
                    dri = (i % 8) // 4 + 1
                    nv = dri + 2
                    sa = 40 + (i % 2) * 12
                    if w1 > 0:
                        tok_sl = P.emit("vector", [(se, tok_ts)],
                               lambda e, s=sa, hf_=hf: e.scalar_tensor_tensor(
                                   stat[:, s + 8:s + 9], stat[:, s + 7:s + 8], kcol[:, hf_:hf_ + 1],
                                   stat[:, s + 6:s + 7], ALU.mult, ALU.add),
                               inc=ve, drain=False)
                        tok_la = P.emit("scalar", [(ve, tok_sl)],
                                        lambda e, s=sa: e.activation(
                                            stat[:, s + 8:s + 9], stat[:, s + 8:s + 9], AF.Identity,
                                            bias=stat[:, s + 11:s + 12], scale=1.0),
                                        inc=se, drain=False)
                        tok_rc = P.emit("vector", [(se, tok_la)],
                               lambda e, s=sa: e.reciprocal(stat[:, s + 9:s + 10], stat[:, s + 8:s + 9]),
                               inc=ve, drain=False)
                    else:
                        P.emit("vector", [(se, tok_ts)],
                               lambda e, s=sa, hf_=hf: e.scalar_tensor_tensor(
                                   stat[:, s + 8:s + 9], stat[:, s + 7:s + 8], kcol[:, hf_:hf_ + 1],
                                   stat[:, s + 11:s + 12], ALU.mult, ALU.add),
                               inc=ve, drain=False)
                        tok_rc = P.emit("vector", [],
                               lambda e, s=sa: e.reciprocal(stat[:, s + 9:s + 10], stat[:, s + 8:s + 9]),
                               inc=ve)
                    tok_rl2 = P.emit("vector", [],
                           lambda e, s=sa, hf_=hf: e.tensor_scalar(
                               stat[:, s + 10:s + 11], stat[:, s + 9:s + 10], kcol[:, hf_:hf_ + 1], None, ALU.mult),
                           inc=ve)
                    P.emit("vector", [(gp, tok_mk)],
                           lambda e, s_=slot, dr_=dri, s=sa: e.tensor_scalar(
                               aoT[:, PH0 + s_ * 2048: PH0 + s_ * 2048 + dr_ * 512],
                               aoT[:, PH0 + s_ * 2048: PH0 + s_ * 2048 + dr_ * 512],
                               stat[:, s + 9:s + 10], None, ALU.mult),
                           inc=ve, drain=False)
                    tok_pn = P.emit("vector", [],
                                    lambda e, s_=slot, dr_=dri, nv_=nv, s=sa: e.tensor_scalar(
                                        aoT[:, PH0 + s_ * 2048 + dr_ * 512: PH0 + s_ * 2048 + nv_ * 512],
                                        aoT[:, PH0 + s_ * 2048 + dr_ * 512: PH0 + s_ * 2048 + nv_ * 512],
                                        stat[:, s + 10:s + 11], None, ALU.mult),
                                    inc=ve)
                    phat_ve[slot] = tok_pn

                class TavState:
                    pass

                def tav_begin(grp):
                    st = TavState()
                    st.grp = grp
                    st.tiles = (2 * grp, 2 * grp + 1)
                    st.hf = grp // 4
                    st.ob = grp % 2
                    dr2 = (st.tiles[1] % 8) // 4 + 1
                    st.klist = [(st.hf * 8 + k, k) for k in range(2 * (grp % 4) + 2)]
                    st.klist += [((1 - st.hf) * 8 + k8, dr2 * 4 + k8) for k8 in range(8)]
                    st.nk = len(st.klist)
                    st.pending = None
                    return st

                def tav_av(st, ki, kseq, ptb, ev):
                    wmm = [ev]
                    if ki == 0:
                        wmm.append((ve, o_ve[st.ob] if o_ve[st.ob] > 0 else csx_tok))
                    def mkav(e, ks=kseq, ob_=st.ob, first=(ki == 0), last=(ki == st.nk - 1), p_=ptb, vb_=vb):
                        e.ldweights(POOL_bf[:, vb_ + ks * 128: vb_ + ks * 128 + 128])
                        return e.matmul(psB[:, ob_ * 512: ob_ * 512 + 256],
                                        POOL_bf[:, vb_ + ks * 128: vb_ + ks * 128 + 128],
                                        POOL_bf[:, PT0 + p_ * 512: PT0 + p_ * 512 + 256],
                                        start=first, stop=last)
                    tok_av = P.emit("tensor", [(sm, v) for (sm, v) in wmm if v > 0], mkav, inc=pe)
                    pt_pe[ptb] = tok_av
                    return tok_av

                def tav_steps(st, k_from, k_to):
                    for ki in range(k_from, k_to):
                        kseq, kph = st.klist[ki]
                        ptb = ki % 2
                        b = tb_next[0]
                        tb_next[0] ^= 1
                        wt = []
                        if tb_ve[b] != 0:
                            wt.append(tb_ve[b])
                        for ti, i in enumerate(st.tiles):
                            slot = i % 4
                            wt2 = list(wt) + [(ve, phat_ve[slot])]
                            tok_t = P.emit("tensor", wt2,
                                           lambda e, s_=slot, kp=kph, b_=b, ti_=ti: e.transpose(
                                               psT[:, b_ * 1024 + ti_ * 128: b_ * 1024 + ti_ * 128 + 128],
                                               aoT[:, PH0 + s_ * 2048 + kp * 128: PH0 + s_ * 2048 + kp * 128 + 128],
                                               ident[:, :]),
                                           inc=pe)
                            phat_pe[slot] = tok_t
                            wt = []
                        tb_pe[b] = tok_t
                        wv = [(pe, tok_t)]
                        if pt_pe[ptb] > 0:
                            wv.append((pe, pt_pe[ptb]))
                        tok_ev = P.emit("vector", wv,
                                        lambda e, b_=b, p_=ptb: e.tensor_copy(
                                            POOL_bf[:, PT0 + p_ * 512: PT0 + p_ * 512 + 256],
                                            psT[:, b_ * 1024: b_ * 1024 + 256]),
                                        inc=ve, drain=False)
                        ev = (ve, tok_ev)
                        tb_ve[b] = ev
                        pt_ve[ptb] = tok_ev
                        if st.pending is not None:
                            tav_av(st, *st.pending)
                        st.pending = (ki, kseq, ptb, ev)

                def tav_end(st):
                    tok_av = tav_av(st, *st.pending)
                    st.pending = None
                    o_pe[st.ob] = tok_av
                    grp = st.grp
                    if st.hf == 0:
                        o_ve[st.ob] = P.emit("vector", [(pe, tok_av)],
                                             lambda e, ha=hloc, g_=grp, ob_=st.ob: e.tensor_copy(
                                                 aoT[:, ha * 1024 + g_ * 256: ha * 1024 + g_ * 256 + 256],
                                                 psB[:, ob_ * 512: ob_ * 512 + 256]),
                                             inc=ve, drain=False)
                    else:
                        sst = hloc % 2
                        wsv = [(pe, tok_av)]
                        if grp == 4 and snd_tok[sst] > 0:
                            wsv.append((s_snd[sst], snd_tok[sst]))
                        o_ve[st.ob] = P.emit("vector", wsv,
                                             lambda e, g_=grp, ob_=st.ob, ss=sst: e.tensor_copy(
                                                 POOL_bf[:, SSTG + ss * 1024 + (g_ - 4) * 256: SSTG + ss * 1024 + (g_ - 4) * 256 + 256],
                                                 psB[:, ob_ * 512: ob_ * 512 + 256]),
                                             inc=ve, drain=False)

                prev = None
                for grp in range(8):
                    e_a = emit_softmax_pre(2 * grp)
                    if prev is not None:
                        tav_steps(prev, 0, prev.nk // 2)
                    e_b = emit_softmax_pre(2 * grp + 1)
                    if prev is not None:
                        tav_steps(prev, prev.nk // 2, prev.nk)
                        tav_end(prev)
                    emit_softmax_post(2 * grp, *e_a)
                    emit_softmax_post(2 * grp + 1, *e_b)
                    prev = tav_begin(grp)
                tav_steps(prev, 0, prev.nk)
                tav_end(prev)

                sst = hloc % 2
                snd_tok[sst] = dma("sync", s_snd[sst], dsend[hloc],
                                   POOL_bf[:, SSTG + sst * 1024: SSTG + sst * 1024 + 1024],
                                   waits=[(ve, o_ve[0]), (ve, o_ve[1])])
            qkv_free_ve[0] = qkv_free_ve[1] = ve.n
            qkv_free_se[0] = qkv_free_se[1] = se.n
            qkv_free_gp[0] = qkv_free_gp[1] = gp.n
        att_done_ve = ve.n
        att_done_se = se.n
        att_done_pe = pe.n

        merge_tok = 0
        if KLIMIT >= 30:
            cc_tok = P.emit("gpsimd", [(s_snd[0], snd_tok[0]), (s_snd[1], snd_tok[1])],
                            lambda e: e.collective_compute(
                                "AllGather", ALU.bypass,
                                replica_groups=[[0, 1], [2, 3], [4, 5], [6, 7]],
                                ins=[dsend], outs=[dgath]),
                            inc=gp)
            r0 = dma("sync", s_rcv, aoT[:, 8192:16384],
                     dgath[0].rearrange("h p f -> p h f"),
                     waits=[(gp, cc_tok), (pe, att_done_pe), (ve, att_done_ve)])
            r1 = dma("sync", s_rcv, POOL_bf[:, 0:8192],
                     dgath[1].rearrange("h p f -> p h f"), waits=[])
            P.emit("vector", [(s_rcv, r1)],
                   lambda e: e.tensor_scalar(aoT[:, 8192:16384], aoT[:, 8192:16384],
                                             kcol[:, 0:1], None, ALU.mult), inc=ve)
            merge_tok = P.emit("vector", [],
                               lambda e: e.scalar_tensor_tensor(
                                   aoT[:, 8192:16384], POOL_bf[:, 0:8192], kcol[:, 1:2],
                                   aoT[:, 8192:16384], ALU.mult, ALU.add), inc=ve)

        WG0, WU0, WD0, WO0 = 0, 4096, 8192, 19456
        gu_tok = [0, 0]
        gu_free = [0, 0]
        wd_tok = [0, 0]
        wd_free = [0, 0]
        wo_tok = [0]
        wo_free = [0]
        xf_tok = [0, 0]
        xf_free = [0, 0]
        gps_ve = [0, 0]
        gps_se = [0, 0]
        hid_read_pe = [0]
        out_tok = [0, 0]
        silu_free = [0]

        for tb in range(2 if KLIMIT >= 40 else 0):
            for ht in range(NC_H):
                ping = ht % 2
                w = [(pe, gu_free[ping])] if gu_tok[ping] > 0 else [(pe, att_done_pe), (se, att_done_se)]
                dma("gpsimd", s_wg[ping], RS_bf[:, WG0 + ping * 2048: WG0 + ping * 2048 + 2048], dwg[ht], waits=w)
                gu_tok[ping] = dma("gpsimd", s_wg[ping], RS_bf[:, WU0 + ping * 2048: WU0 + ping * 2048 + 2048], dwu[ht])
                gcol = ping * 1024
                ucol = ping * 1024 + 512
                w = [(s_wg[ping], gu_tok[ping]), (ve, csx_tok), (ve, xnT_tok), (ve, att_done_ve)]
                if gps_ve[ping] > 0:
                    w += [(ve, gps_ve[ping]), (se, gps_se[ping])]
                for c in range(NC_D):
                    def mkg(e, c_=c, g_=gcol, p_=ping, tb_=tb):
                        e.ldweights(RS_bf[:, WG0 + p_ * 2048 + c_ * 128: WG0 + p_ * 2048 + c_ * 128 + 128])
                        return e.matmul(psA[:, g_:g_ + 512],
                                        RS_bf[:, WG0 + p_ * 2048 + c_ * 128: WG0 + p_ * 2048 + c_ * 128 + 128],
                                        xnT[:, c_ * T + tb_ * 512: c_ * T + tb_ * 512 + 512],
                                        start=(c_ == 0), stop=False)
                    P.emit("tensor", w if c == 0 else [], mkg, inc=None)
                P.emit("tensor", [], lambda e, g_=gcol, tb_=tb: e.matmul(
                    psA[:, g_:g_ + 512], ones_r[0:1, :], csx_hi[0:1, tb_ * 512: tb_ * 512 + 512], start=False, stop=False), inc=None)
                P.emit("tensor", [], lambda e, g_=gcol, tb_=tb: e.matmul(
                    psA[:, g_:g_ + 512], ones_r[0:1, :], csx_lo[0:1, tb_ * 512: tb_ * 512 + 512], start=False, stop=True), inc=None)
                for c in range(NC_D):
                    def mku(e, c_=c, u_=ucol, p_=ping, tb_=tb):
                        e.ldweights(RS_bf[:, WU0 + p_ * 2048 + c_ * 128: WU0 + p_ * 2048 + c_ * 128 + 128])
                        return e.matmul(psA[:, u_:u_ + 512],
                                        RS_bf[:, WU0 + p_ * 2048 + c_ * 128: WU0 + p_ * 2048 + c_ * 128 + 128],
                                        xnT[:, c_ * T + tb_ * 512: c_ * T + tb_ * 512 + 512],
                                        start=(c_ == 0), stop=False)
                    P.emit("tensor", [], mku, inc=None)
                P.emit("tensor", [], lambda e, u_=ucol, tb_=tb: e.matmul(
                    psA[:, u_:u_ + 512], ones_r[0:1, :], csx_hi[0:1, tb_ * 512: tb_ * 512 + 512], start=False, stop=False), inc=None)
                tok_up = P.emit("tensor", [], lambda e, u_=ucol, tb_=tb: e.matmul(
                    psA[:, u_:u_ + 512], ones_r[0:1, :], csx_lo[0:1, tb_ * 512: tb_ * 512 + 512], start=False, stop=True), inc=pe)
                gu_free[ping] = tok_up
                tok_sg = P.emit("scalar", [(pe, tok_up), (ve, silu_free[0])],
                                lambda e, g_=gcol, ht_=ht: e.activation(
                                    RS[:, SILU_G:SILU_G + 512], psA[:, g_:g_ + 512], AF.Silu, scale=sgc[:, ht_:ht_ + 1]),
                                inc=se, drain=False)
                tok_hid = P.emit("vector", [(se, tok_sg), (pe, hid_read_pe[0]), (ve, merge_tok)],
                                 lambda e, u_=ucol, ht_=ht: e.scalar_tensor_tensor(
                                     POOL_bf[:, HID0 + ht_ * 512: HID0 + ht_ * 512 + 512],
                                     psA[:, u_:u_ + 512], suc[:, ht_:ht_ + 1],
                                     RS[:, SILU_G:SILU_G + 512], ALU.mult, ALU.mult),
                                 inc=ve, drain=False)
                gps_ve[ping] = tok_hid
                gps_se[ping] = tok_sg
                silu_free[0] = tok_hid
            hid_all = ve.n

            w0 = [(ve, hid_all)]
            if o_ve[0] > 0:
                w0 += [(ve, o_ve[0]), (ve, o_ve[1])]
            for c in range(NC_H):
                P.emit("tensor", w0 if c == 0 else [],
                       lambda e, c_=c: e.matmul(psB[0:1, 0:512], ones_c[:, :],
                                                POOL_bf[:, HID0 + c_ * 512: HID0 + c_ * 512 + 512],
                                                start=(c_ == 0), stop=(c_ == NC_H - 1)),
                       inc=pe if c == NC_H - 1 else None)
            tok_csh = P.emit("vector", [(pe, pe.n)],
                             lambda e: e.tensor_copy(csh[:, :], psB[0:1, 0:512]), inc=ve)
            tok_n = P.emit("vector", [(ve, tok_csh)],
                           lambda e: e.tensor_scalar(csh[:, :], csh[:, :], -128.0, None, ALU.mult), inc=ve)
            tok_hi2 = P.emit("vector", [(ve, tok_n)],
                             lambda e: e.tensor_copy(csh_hi[:, :], csh[:, :]), inc=ve)
            csh_tok = P.emit("vector", [(ve, tok_hi2)],
                             lambda e: e.tensor_tensor(csh_lo[:, :], csh[:, :], csh_hi[:, :], ALU.subtract), inc=ve)

            for dt in range(16):
                ping = dt % 2
                w = [(pe, wd_free[ping])] if wd_tok[ping] > 0 else []
                wd_tok[ping] = dma("gpsimd", s_wd[ping], RS_bf[:, WD0 + ping * 5632: WD0 + ping * 5632 + 5632], dwd[dt], waits=w)
                w = [(pe, wo_free[0])] if wo_tok[0] > 0 else []
                wo_tok[0] = dma("gpsimd", s_wo, RS_bf[:, WO0:WO0 + 2048], dwout[dt], waits=w)
                if xf_tok[ping] > 0:
                    w = [(ve, xf_free[ping]), (s_out[ping], out_tok[ping])]
                else:
                    w = [(pe, att_done_pe)]
                xf_tok[ping] = dma("sync", s_xf[ping], RS[:, XF0 + ping * 512: XF0 + ping * 512 + 512],
                                   dxT[dt * 128: dt * 128 + 128, tb * 512: tb * 512 + 512], waits=w)

                gp_col = ping * 1024
                sp_col = ping * 1024 + 512
                wmm = [(s_wd[ping], wd_tok[ping]), (ve, csh_tok)]
                if gps_ve[ping] > 0:
                    wmm += [(ve, gps_ve[ping]), (se, gps_se[ping])]
                for c in range(NC_H):
                    def mkd(e, c_=c, g_=gp_col, p_=ping):
                        e.ldweights(RS_bf[:, WD0 + p_ * 5632 + c_ * 128: WD0 + p_ * 5632 + c_ * 128 + 128])
                        return e.matmul(psA[:, g_:g_ + 512],
                                        RS_bf[:, WD0 + p_ * 5632 + c_ * 128: WD0 + p_ * 5632 + c_ * 128 + 128],
                                        POOL_bf[:, HID0 + c_ * 512: HID0 + c_ * 512 + 512],
                                        start=(c_ == 0), stop=False)
                    P.emit("tensor", wmm if c == 0 else [], mkd, inc=None)
                P.emit("tensor", [], lambda e, g_=gp_col: e.matmul(
                    psA[:, g_:g_ + 512], ones_r[0:1, :], csh_hi[0:1, :], start=False, stop=False), inc=None)
                tok_d = P.emit("tensor", [], lambda e, g_=gp_col: e.matmul(
                    psA[:, g_:g_ + 512], ones_r[0:1, :], csh_lo[0:1, :], start=False, stop=True), inc=pe)
                hid_read_pe[0] = tok_d
                wd_free[ping] = tok_d
                for c in range(NC_D):
                    def mko(e, c_=c, s_=sp_col, tb_=tb):
                        e.ldweights(RS_bf[:, WO0 + c_ * 128: WO0 + c_ * 128 + 128])
                        return e.matmul(psA[:, s_:s_ + 512],
                                        RS_bf[:, WO0 + c_ * 128: WO0 + c_ * 128 + 128],
                                        aoT[:, c_ * 1024 + tb_ * 512: c_ * 1024 + tb_ * 512 + 512],
                                        start=(c_ == 0), stop=(c_ == NC_D - 1))
                    P.emit("tensor", [(s_wo, wo_tok[0]), (ve, att_done_ve), (ve, merge_tok)] if c == 0 else [], mko,
                           inc=pe if c == NC_D - 1 else None)
                tok_s = pe.n
                wo_free[0] = tok_s
                tok_g = P.emit("scalar", [(pe, tok_d), (ve, silu_free[0])],
                               lambda e, g_=gp_col, dt_=dt: e.activation(
                                   RS[:, SILU_G:SILU_G + 512],
                                   psA[:, g_:g_ + 512], AF.Silu, scale=sdc[:, dt_:dt_ + 1]),
                               inc=se, drain=False)
                tok_p = P.emit("vector", [(se, tok_g), (pe, tok_s)],
                               lambda e, s_=sp_col: e.tensor_tensor(
                                   RS[:, SILU_G:SILU_G + 512], psA[:, s_:s_ + 512],
                                   RS[:, SILU_G:SILU_G + 512], ALU.mult),
                               inc=ve, drain=False)
                tok_o = P.emit("vector", [(ve, tok_p), (s_xf[ping], xf_tok[ping])],
                               lambda e, p_=ping: e.tensor_tensor(
                                   RS[:, XF0 + p_ * 512: XF0 + p_ * 512 + 512],
                                   RS[:, SILU_G:SILU_G + 512],
                                   RS[:, XF0 + p_ * 512: XF0 + p_ * 512 + 512], ALU.add),
                               inc=ve)
                xf_free[ping] = tok_o
                silu_free[0] = tok_o
                gps_ve[ping] = tok_o
                gps_se[ping] = tok_g
                out_tok[ping] = dma("sync", s_out[ping], dout[dt * 128: dt * 128 + 128, tb * 512: tb * 512 + 512],
                                    RS[:, XF0 + ping * 512: XF0 + ping * 512 + 512], waits=[(ve, tok_o)])

        if DEBUG:
            tdbg = dma("sync", s_cos, d_xnT, xnT[:, :], waits=[(ve, xnT_tok)])
            tdbg = dma("sync", s_cos, d_aoT, aoT[:, :], waits=[(ve, att_done_ve)])
            P.emit("sync", [(s_cos, tdbg)], lambda e: e.nop(), inc=None)
        P.emit("sync", [(s_out[0], out_tok[0]), (s_out[1], out_tok[1])], lambda e: e.nop(), inc=None)

        with nc.Block() as block:
            @block.sync
            def _(e):
                for f in P.ops["sync"]:
                    f(e)

            @block.gpsimd
            def _(e):
                for f in P.ops["gpsimd"]:
                    f(e)

            @block.tensor
            def _(e):
                for f in P.ops["tensor"]:
                    f(e)

            @block.vector
            def _(e):
                for f in P.ops["vector"]:
                    f(e)

            @block.scalar
            def _(e):
                for f in P.ops["scalar"]:
                    f(e)

    return nc


def _host_prep(inputs):
    x = np.asarray(inputs["L0_attn_x"], dtype=np.float32)
    w_qkv = np.asarray(inputs["L0_attn_w_qkv"], dtype=np.float32)
    w_out = np.asarray(inputs["L0_attn_w_out"], dtype=np.float32)
    qg = np.asarray(inputs["L0_attn_q_gamma"], dtype=np.float32)
    kg = np.asarray(inputs["L0_attn_k_gamma"], dtype=np.float32)
    wqg = np.asarray(inputs["L0_ffn_wq_gate"])
    wqu = np.asarray(inputs["L0_ffn_wq_up"])
    wqd = np.asarray(inputs["L0_ffn_wq_down"])
    sg = np.asarray(inputs["L0_ffn_s_gate"], dtype=np.float32)
    su = np.asarray(inputs["L0_ffn_s_up"], dtype=np.float32)
    sd = np.asarray(inputs["L0_ffn_s_down"], dtype=np.float32)

    wq = w_qkv[:, 0:2048].reshape(D, NH, HD)
    wk = w_qkv[:, 2048:4096].reshape(D, NH, HD)
    wv = w_qkv[:, 4096:6144].reshape(D, NH, HD)
    wqkv_all = np.empty((8, D, 768), dtype=np.float32)
    for p in range(8):
        wqkv_all[p, :, 0:128] = wk[:, 2 * p]
        wqkv_all[p, :, 128:256] = wk[:, 2 * p + 1]
        wqkv_all[p, :, 256:384] = wv[:, 2 * p]
        wqkv_all[p, :, 384:512] = wv[:, 2 * p + 1]
        wqkv_all[p, :, 512:640] = wq[:, 2 * p]
        wqkv_all[p, :, 640:768] = wq[:, 2 * p + 1]
    def wqkv_for(g):
        wt = wqkv_all[4 * g:4 * g + 4]
        return np.ascontiguousarray(
            wt.reshape(2, 2, NC_D, 128, 768).transpose(0, 3, 1, 2, 4))
    def wout_for(g):
        blk = w_out.reshape(16, 128, D)
        order = list(range(8 * g, 8 * g + 8)) + list(range(8 * (1 - g), 8 * (1 - g) + 8))
        wp = blk[order].reshape(D, D)
        return np.ascontiguousarray(wp.reshape(16, 128, 16, 128).transpose(2, 1, 0, 3))

    def u8(a):
        return a.astype(np.int32).view(np.uint8)[..., 0::4]

    wgT = np.ascontiguousarray(u8(wqg).T)
    wuT = np.ascontiguousarray(u8(wqu).T)
    wdT = np.ascontiguousarray(u8(wqd).T)
    wg8 = np.ascontiguousarray(wgT.reshape(16, 128, NC_H, 128).transpose(2, 1, 0, 3))
    wu8 = np.ascontiguousarray(wuT.reshape(16, 128, NC_H, 128).transpose(2, 1, 0, 3))
    wd8 = np.ascontiguousarray(wdT.reshape(NC_H, 128, 16, 128).transpose(2, 1, 0, 3))

    sgc = np.ascontiguousarray(sg.reshape(NC_H, 128).T)
    suc = np.ascontiguousarray(su.reshape(NC_H, 128).T)
    sdc = np.ascontiguousarray(sd.reshape(16, 128).T)
    qgb = np.ascontiguousarray(np.broadcast_to(qg, (128, 128)))
    kgb = np.ascontiguousarray(np.broadcast_to(kg, (128, 128)))

    inv_freq = 1.0 / (10000.0 ** (np.arange(0, HD, 2, dtype=np.float64) / HD))
    tpos = np.arange(T, dtype=np.float64)
    freqs = tpos[:, None] * inv_freq[None, :]
    emb = np.concatenate([freqs, freqs], axis=1)
    cos_t = np.cos(emb)
    ssin_t = np.sin(emb)
    ssin_t[:, 0:64] = -ssin_t[:, 0:64]

    diagm = np.zeros((128, 640), dtype=np.float32)
    qi = np.arange(128)[:, None]
    ki = np.arange(128)[None, :]
    diagm[:, 0:128] = np.where(qi >= ki, 1.0, 0.0)
    diagm[:, 128:640] = 0.0
    diagm = diagm.astype(ml_dtypes.bfloat16)

    ident = np.eye(128, dtype=np.float32).astype(ml_dtypes.bfloat16)

    per_core = []
    for core in range(8):
        b, g = core // 2, core % 2
        xb = x[b]
        if g == 0:
            xp = xb
            pos = np.arange(T)
        else:
            xp = np.ascontiguousarray(np.concatenate([xb[1024:], xb[:1024]], axis=0))
            pos = np.concatenate([np.arange(1024, 2048), np.arange(1024)])
        cosp = np.ascontiguousarray(cos_t[pos].astype(np.float16).reshape(16, 128, 128).transpose(1, 0, 2))
        sinp = np.ascontiguousarray(ssin_t[pos].astype(np.float16).reshape(16, 128, 128).transpose(1, 0, 2))
        kc = np.ascontiguousarray(np.broadcast_to(
            np.array([float(g), float(1 - g)], dtype=np.float32), (128, 2)))
        xT_own = np.ascontiguousarray(xp[:1024].T)
        per_core.append(dict(
            x=np.ascontiguousarray(xp), xT=xT_own,
            wqkv=wqkv_for(g), wout=wout_for(g), wg8=wg8, wu8=wu8, wd8=wd8,
            sgc=sgc, suc=suc, sdc=sdc, qgb=qgb, kgb=kgb,
            cosq=cosp, ssin=sinp, diagm=diagm, ident=ident, kcol=kc,
        ))
    return per_core


def kernel(**inputs):
    per_core = _host_prep(inputs)
    nc = build_program()
    res = run_bass_kernel_spmd(nc, per_core, core_ids=list(range(8)))
    out = np.empty((4, 2048, 2048), dtype=np.float32)
    for core in range(8):
        b, g = core // 2, core % 2
        shard = res.results[core]["out"]
        out[b, g * 1024:(g + 1) * 1024, :] = np.asarray(shard).T
    return out
